# version 06
# baseline (speedup 1.0000x reference)
import numpy as np
import concourse.bass as bass
import concourse.mybir as mybir
from concourse.bass_utils import run_bass_kernel_spmd
from contextlib import ExitStack

F32 = mybir.dt.float32
BF16 = mybir.dt.bfloat16
I32 = mybir.dt.int32
AF = mybir.ActivationFunctionType
ALU = mybir.AluOpType
AX = mybir.AxisListType.X

T_ALL, T_OWN, D, H, E, S, FUVZ = 16384, 2048, 1024, 3072, 2048, 128, 4224
LSEQ = 16384.0
EPS = 1e-5
NC_N = 8
ST = 512
NST = T_ALL // ST
NST_OWN = T_OWN // ST
NTILE = T_ALL // 128
DQ = 1.0 / 127.0
SB_BASE = 16512
SB_LIMIT = 229376


class Sem:
    __slots__ = ("h", "n", "name")


class Mile:
    __slots__ = ("name", "hist", "targets")

    def __init__(self, name):
        self.name = name
        self.hist = []
        self.targets = {}

    def rec(self, h, n):
        self.hist.append((h, n))
        self.targets[id(h)] = (h, n)

    def back(self, k):
        return self.hist[-1 - k] if len(self.hist) > k else None

    def all(self):
        return list(self.targets.values())


class Builder:
    CHAINED = ("vector", "scalar", "gpsimd")

    def __init__(self, nc, ctx):
        self.nc, self.ctx = nc, ctx
        self.streams = {k: [] for k in ("sync", "gpsimd", "tensor", "vector", "scalar")}
        self.pending = {k: [] for k in self.streams}
        self.all_sems = []
        self.chains = {}

    def sem(self, name):
        s = Sem()
        s.h = self.ctx.enter_context(self.nc.semaphore(name))
        s.n = 0
        s.name = name
        self.all_sems.append(s)
        return s

    def snap(self):
        return [(s, s.n) for s in self.all_sems if s.n > 0]

    def barrier(self, snap=None):
        snap = self.snap() if snap is None else snap
        for k in self.pending:
            self.pending[k] = list(snap)

    def op(self, eng, builder, waits=(), incs=(), dma=False):
        wl = {}

        def add(h, v):
            if v > 0:
                wl[id(h)] = (h, max(wl[id(h)][1], v) if id(h) in wl else v)

        for w in list(self.pending[eng]) + list(waits):
            if isinstance(w, Sem):
                add(w.h, w.n)
            elif isinstance(w, Mile):
                for h, v in w.all():
                    add(h, v)
            elif isinstance(w, tuple) and isinstance(w[0], Sem):
                add(w[0].h, w[1])
            elif isinstance(w, tuple) and isinstance(w[0], Mile):
                b = w[0].back(w[1])
                if b is not None:
                    add(b[0], b[1])
            elif isinstance(w, tuple) and len(w) == 2:
                add(w[0], w[1])
            else:
                raise TypeError(f"bad wait {w}")
        self.pending[eng] = []

        chained = eng in self.chains and not dma
        il = []
        if chained:
            ch = self.chains[eng]
            if ch.n > 0:
                add(ch.h, ch.n)
            ch.n += 1
            il.append((ch.h, 1))
            for i in incs:
                m = i if not isinstance(i, tuple) else i[0]
                assert isinstance(m, Mile), f"chained op can only inc Mile, got {m}"
                m.rec(ch.h, ch.n)
        else:
            for i in incs:
                s2, k = i if isinstance(i, tuple) else (i, 1)
                assert isinstance(s2, Sem), f"unchained op needs Sem inc, got {s2}"
                s2.n += k
                il.append((s2.h, k))
            assert len(il) <= 1, "one sem update per instruction"
        wl = list(wl.values())

        def f(e, b=builder, wl=wl, il=il):
            for h, v in wl:
                e.wait_ge(h, v)
            ins = b(e)
            for h, k in il:
                ins.then_inc(h, k)
        self.streams[eng].append(f)


def build_nc(dbg=False, upto=5):
    nc = bass.Bass("TRN2", target_bir_lowering=False, debug=False)
    ctx = ExitStack()

    x_in = nc.dram_tensor("L0_attn_x", [T_ALL, D], F32, kind="ExternalInput").ap()
    ln_g = nc.dram_tensor("L0_attn_ln_g", [D], F32, kind="ExternalInput").ap()
    ln_b = nc.dram_tensor("L0_attn_ln_b", [D], F32, kind="ExternalInput").ap()
    W_in = nc.dram_tensor("L0_attn_W_in", [D, FUVZ], F32, kind="ExternalInput").ap()
    b_in = nc.dram_tensor("L0_attn_b_in", [FUVZ], F32, kind="ExternalInput").ap()
    W_out = nc.dram_tensor("L0_attn_W_out", [E, D], F32, kind="ExternalInput").ap()
    b_out = nc.dram_tensor("L0_attn_b_out", [D], F32, kind="ExternalInput").ap()
    gam_q = nc.dram_tensor("L0_attn_gamma_q", [S], F32, kind="ExternalInput").ap()
    bet_q = nc.dram_tensor("L0_attn_beta_q", [S], F32, kind="ExternalInput").ap()
    gam_k = nc.dram_tensor("L0_attn_gamma_k", [S], F32, kind="ExternalInput").ap()
    bet_k = nc.dram_tensor("L0_attn_beta_k", [S], F32, kind="ExternalInput").ap()
    wg_i = nc.dram_tensor("L0_ffn_w_gate", [H, D], I32, kind="ExternalInput").ap()
    wu_i = nc.dram_tensor("L0_ffn_w_up", [H, D], I32, kind="ExternalInput").ap()
    wd_i = nc.dram_tensor("L0_ffn_w_down", [D, H], I32, kind="ExternalInput").ap()
    out = nc.dram_tensor("out", [T_OWN, D], F32, kind="ExternalOutput").ap()

    def dscr(name, shape, dtype, as_out=False):
        kind = "ExternalOutput" if (dbg and as_out) else "Internal"
        return nc.dram_tensor(name, shape, dtype, kind=kind).ap()

    x1_dram = dscr("dbg_x1", [T_ALL, D], F32, as_out=True)
    v_dram = dscr("dbg_v", [128, NTILE * E], BF16, as_out=True)
    k_dram = dscr("dbg_k", [S, T_ALL], BF16, as_out=True)
    u_dram = dscr("dbg_u", [E, T_OWN], BF16, as_out=True)
    xm_dram = dscr("dbg_xm", [T_OWN, D], F32, as_out=True)
    x2_dram = dscr("dbg_x2", [T_OWN, D], F32, as_out=True)
    wgT_dram = dscr("wgT_dram", [128, 8 * H], BF16)
    wuT_dram = dscr("wuT_dram", [128, 8 * H], BF16)
    wdT_dram = dscr("wdT_dram", [128, 24 * D], BF16)

    class Cur:
        def __init__(self, base):
            self.off = base

        def al(self, name, shape, dtype):
            nbytes = int(np.prod(shape[1:])) * mybir.dt.size(dtype)
            o = self.off
            self.off = (self.off + nbytes + 31) & ~31
            assert self.off <= SB_LIMIT, f"SBUF overflow at {name}: {self.off}"
            return nc.alloc_sbuf_tensor_at(name, list(shape), dtype, offset=o).ap()

    cc = Cur(SB_BASE)
    ident = cc.al("ident", [128, 128], BF16)
    ones = cc.al("ones", [1, 512], BF16)
    binrow = cc.al("binrow", [1, FUVZ], BF16)
    bout_bf = cc.al("bout_bf", [1, D], BF16)
    bz_t = cc.al("bz", [128, 1], F32)
    bu_t = cc.al("bu", [128, 16], F32)
    gq_t = cc.al("gq", [128, 1], F32)
    bq_t = cc.al("bq", [128, 1], F32)
    gk_t = cc.al("gk", [128, 1], F32)
    bk_t = cc.al("bk", [128, 1], F32)
    lng_t = cc.al("lng", [128, 8], F32)
    lnb_t = cc.al("lnb", [128, 8], F32)
    eps_t = cc.al("eps", [128, 1], F32)
    sum1 = cc.al("sum1", [128, NTILE], F32)
    ssq1 = cc.al("ssq1", [128, NTILE], F32)
    nmu1 = cc.al("nmu1", [128, NTILE], F32)
    rst1 = cc.al("rst1", [128, NTILE], F32)
    sum2 = cc.al("sum2", [128, NTILE], F32)
    ssq2 = cc.al("ssq2", [128, NTILE], F32)
    nmu2 = cc.al("nmu2", [128, NTILE], F32)
    rst2 = cc.al("rst2", [128, NTILE], F32)
    sfold = cc.al("sfold", [128, NTILE], F32)
    stat_t = cc.al("stat_t", [128, NTILE], F32)
    sum3 = cc.al("sum3", [128, 16], F32)
    ssq3 = cc.al("ssq3", [128, 16], F32)
    nmu3 = cc.al("nmu3", [128, 16], F32)
    rst3 = cc.al("rst3", [128, 16], F32)
    nb1 = cc.al("nb1", [128, NTILE], F32)
    nbf = cc.al("nbf", [128, NTILE], F32)
    nb3 = cc.al("nb3", [128, 16], F32)
    q_sb = cc.al("q_sb", [128, T_OWN], BF16)
    PB = (cc.off + 63) & ~63

    ca = Cur(PB)
    wgT = ca.al("wgT", [128, 8, H], BF16)
    wuT = ca.al("wuT", [128, 8, H], BF16)
    SCR = ca.off
    c = Cur(SCR)
    wd_bf = c.al("wd_bf", [128, 8, H // 4], BF16)
    wint = c.al("wint", [128, 2, 3, D], I32)
    wbf = c.al("wbf", [128, 2, 3, D], BF16)
    rowst = c.al("rowst", [128, 2, 8, 128], BF16)
    xs_t = c.al("xs_t", [128, 2, 4, D], F32)
    sq0 = c.al("sq0", [128, D], F32)
    c = Cur(SCR)
    wdTr = c.al("wdTr", [128, 12, D], BF16)
    h_sb = c.al("h_sb", [128, 12, ST], BF16)
    g_bf = c.al("g_bf", [128, 2, ST], BF16)
    xn_t = c.al("xn_t", [128, 4, D], F32)
    x1a = c.al("x1a", [128, 4, D], F32)
    xn1tm = c.al("xn1tm", [128, 2, D], BF16)
    xn1fm = c.al("xn1fm", [128, 8, ST], BF16)
    sq1a = c.al("sq1a", [128, D], F32)
    c = Cur(PB)
    win_sb = c.al("win_sb", [128, 8, FUVZ], BF16)
    x1r = c.al("x1r", [128, 4, D], F32)
    zn_tm = c.al("zn_tm", [128, 2, D], BF16)
    xnifm = c.al("xnifm", [128, 2, 8, ST], BF16)
    v_stg = c.al("v_stg", [128, 2, E], BF16)
    z_scr = c.al("z_scr", [128, 2, ST], F32)
    k_stg = c.al("k_stg", [128, 2, ST], BF16)
    u_stg = c.al("u_stg", [128, 4, ST], BF16)
    xm_stg = c.al("xm_stg", [128, 2, D], F32)
    xm_tmp = c.al("xm_tmp", [128, D], F32)
    c = Cur(PB)
    o_sb = c.al("o_sb", [128, 16, 1024], F32)
    ALIAS = c.off
    vring = c.al("vring", [128, 16, E], BF16)
    p_sb = c.al("p_sb", [128, 2, 8, 1024], BF16)
    rl_scr = c.al("rl_scr", [128, 4, 512], F32)
    kst = c.al("kst", [128, 2, 1024], BF16)
    c = Cur(ALIAS)
    wout_sb = c.al("wout_sb", [128, 16, 1024], BF16)
    ou_bf = c.al("ou_bf", [128, 16, 1024], BF16)
    u_ld = c.al("u_ld", [128, 2, 1024], BF16)
    xm_ld = c.al("xm_ld", [128, 2, D], F32)
    x2_stg = c.al("x2_stg", [128, 2, D], F32)
    sq3_scr = c.al("sq3", [128, D], F32)

    pA = ctx.enter_context(nc.psum_tensor("pA", [128, 512], F32))
    pB = ctx.enter_context(nc.psum_tensor("pB", [128, 512], F32))
    pC = ctx.enter_context(nc.psum_tensor("pC", [128, 1024], F32))
    pD = ctx.enter_context(nc.psum_tensor("pD", [128, 1024], F32))
    pT = ctx.enter_context(nc.psum_tensor("pT", [128, 1024], BF16))
    pT2 = ctx.enter_context(nc.psum_tensor("pT2", [128, 1024], BF16))
    pTv = [pT, pT2]

    B = Builder(nc, ctx)
    B.chains = {k: B.sem("chain_" + k) for k in ("vector", "scalar", "gpsimd")}
    sm = {n: B.sem(n) for n in (
        "sC", "sCB", "sWI", "sWO", "sR",
        "tX1", "tOUT",
        "cTP", "cGA", "cUP", "cDWN", "cPE1", "cSC", "cPV", "cY")}
    for base, k in (("sW", 2), ("sXs", 2), ("sXr", 4), ("sQ", 4), ("sWD", 3),
                    ("sV", 2), ("sK", 2), ("sU", 2), ("sM", 2),
                    ("tV", 2), ("tK", 2), ("tU", 4), ("tM", 2), ("tX2", 2),
                    ("tWT", 3)):
        sm[base] = [B.sem(f"{base}{i}") for i in range(k)]
    for n in ("cDEQ", "cTE", "cNRM", "cSIL", "cH", "cADD", "cSTA", "cRSD",
              "cVE", "cKQ", "cRL", "cP", "cOE", "cOU", "cX2", "cMISC",
              "cSF1", "cSF2", "cDCm", "cNB"):
        sm[n] = Mile(n)
    g = lambda n: sm[n]

    def ld_const(dst, src):
        B.op("gpsimd", lambda e, d=dst, s=src: e.dma_start(out=d, in_=s),
             incs=[(g("sC"), 16)], dma=True)

    ld_const(bz_t[:, 0], b_in[2 * E:2 * E + S])
    for ecn in range(16):
        ld_const(bu_t[:, ecn], b_in[ecn * 128:(ecn + 1) * 128])
    ld_const(gq_t[:, 0], gam_q)
    ld_const(bq_t[:, 0], bet_q)
    ld_const(gk_t[:, 0], gam_k)
    ld_const(bk_t[:, 0], bet_k)
    for cn in range(8):
        ld_const(lng_t[:, cn], ln_g[cn * 128:(cn + 1) * 128])
        ld_const(lnb_t[:, cn], ln_b[cn * 128:(cn + 1) * 128])
    B.op("gpsimd", lambda e: e.dma_start(
        out=binrow[0:1, :], in_=b_in.rearrange("(a f) -> a f", a=1)),
        incs=[(g("sCB"), 16)], dma=True)
    B.op("gpsimd", lambda e: e.dma_start(
        out=bout_bf[0:1, :], in_=b_out.rearrange("(a f) -> a f", a=1)),
        incs=[(g("sCB"), 16)], dma=True)
    NCONST = g("sC").n

    B.op("gpsimd", lambda e: e.memset(ident[:], 0.0))
    B.op("gpsimd", lambda e: e.affine_select(
        out=ident[:], in_=ident[:], compare_op=ALU.not_equal, fill=1.0,
        base=0, pattern=[[-1, 128]], channel_multiplier=1), incs=[g("cMISC")])
    B.op("gpsimd", lambda e: e.memset(ones[:], 1.0), incs=[g("cMISC")])
    B.op("vector", lambda e: e.memset(eps_t[:], EPS), incs=[g("cMISC")])
    B.op("scalar", lambda e: e.mul(gq_t[:], gq_t[:], 1.0 / LSEQ),
         waits=[(g("sC"), NCONST)], incs=[g("cMISC")])
    B.op("scalar", lambda e: e.mul(bq_t[:], bq_t[:], 1.0 / LSEQ), incs=[g("cMISC")])
    IDENT_OK = g("cMISC").all()

    def prep_gu(w_src, wT_dst):
        for grp in range(8):
            sl = grp % 2
            B.op("gpsimd", lambda e, grp=grp, sl=sl, w=w_src: e.dma_start(
                out=wint[:, sl],
                in_=w[grp * 384:(grp + 1) * 384, :].rearrange("(c p) f -> p c f", p=128)),
                waits=[(g("cDEQ"), 1)], incs=[(g("sW")[sl], 16)], dma=True)
            B.op("vector", lambda e, sl=sl: e.tensor_scalar_mul(
                wbf[:, sl], wint[:, sl], DQ),
                waits=[g("sW")[sl], (g("cTP"), max(0, g("cTP").n - 2))],
                incs=[g("cDEQ")])
            for j in range(3):
                r = grp * 3 + j
                pt = pTv[r % 2]
                for fc in range(8):
                    B.op("tensor", lambda e, sl=sl, j=j, fc=fc, pt=pt: e.transpose(
                        pt[:].rearrange("p (f j2) -> p f j2", f=8)[:, fc],
                        wbf[:, sl, j, fc * 128:(fc + 1) * 128], ident[:]),
                        waits=(IDENT_OK + [g("cDEQ"), (g("cTE"), 1)] if fc == 0 else []),
                        incs=([g("cTP")] if fc == 7 else []))
                B.op("vector", lambda e, r=r, pt=pt, wT=wT_dst: e.tensor_copy(
                    wT[:, :, r * 128:(r + 1) * 128],
                    pt[:].rearrange("p (f j2) -> p f j2", f=8)),
                    waits=[g("cTP")], incs=[g("cTE")])

    prep_gu(wg_i, wgT)
    prep_gu(wu_i, wuT)
    B.op("gpsimd", lambda e: e.dma_start(
        out=wgT_dram, in_=wgT[:].rearrange("p c h -> p (c h)")),
        waits=[g("cTE")], incs=[(g("tWT")[2], 16)], dma=True)
    B.op("gpsimd", lambda e: e.dma_start(
        out=wuT_dram, in_=wuT[:].rearrange("p c h -> p (c h)")),
        incs=[(g("tWT")[2], 16)], dma=True)

    for ch in range(4):
        tp_guard = (g("cTP"), g("cTP").n)
        for r in range(8):
            sl = r % 2
            B.op("gpsimd", lambda e, r=r, sl=sl, ch=ch: e.dma_start(
                out=wint[:, sl].rearrange("p c f -> p (c f)")[:, 0:768],
                in_=wd_i[r * 128:(r + 1) * 128, ch * 768:(ch + 1) * 768]),
                waits=[(g("cDEQ"), 1)], incs=[(g("sW")[sl], 16)], dma=True)
            B.op("vector", lambda e, r=r, sl=sl: e.tensor_scalar_mul(
                wd_bf[:, r, :], wint[:, sl].rearrange("p c f -> p (c f)")[:, 0:768],
                0.5 * DQ),
                waits=[g("sW")[sl], tp_guard], incs=[g("cDEQ")])
        WD_DEQ = g("cDEQ").back(0)
        for ci in range(6):
            cn = ch * 6 + ci
            sl = cn % 2
            pt = pTv[sl]
            for r in range(8):
                B.op("tensor", lambda e, ci=ci, r=r, pt=pt: e.transpose(
                    pt[:].rearrange("p (f j) -> p f j", f=8)[:, r],
                    wd_bf[:, r, ci * 128:(ci + 1) * 128], ident[:]),
                    waits=([WD_DEQ, (g("cTE"), 1)] if r == 0 else []),
                    incs=([g("cTP")] if r == 7 else []))
            B.op("vector", lambda e, sl=sl, pt=pt: e.tensor_copy(
                rowst[:, sl], pt[:].rearrange("p (f j) -> p f j", f=8)),
                waits=[g("cTP"), g("tWT")[sl]], incs=[g("cTE")])
            B.op("gpsimd", lambda e, cn=cn, sl=sl: e.dma_start(
                out=wdT_dram[:, cn * D:(cn + 1) * D],
                in_=rowst[:, sl].rearrange("p r j -> p (r j)")),
                waits=[g("cTE")], incs=[(g("tWT")[sl], 16)], dma=True)
    TWT_ALL = [(t, t.n) for t in g("tWT")]

    STA_MID = None
    for grp in range(NTILE // 4):
        sl = grp % 2
        B.op("sync", lambda e, grp=grp, sl=sl: e.dma_start(
            out=xs_t[:, sl],
            in_=x_in[grp * 512:(grp + 1) * 512, :].rearrange("(c p) f -> p c f", p=128)),
            waits=[(g("cSTA"), 8), (g("cSTA"), 9)], incs=[(g("sXs")[sl], 16)])
        for j in range(4):
            t = grp * 4 + j
            B.op("vector", lambda e, sl=sl, t=t, j=j: e.reduce_sum(
                sum1[:, t:t + 1], xs_t[:, sl, j], axis=AX), waits=[g("sXs")[sl]],
                incs=[g("cSTA")])
            B.op("scalar", lambda e, sl=sl, t=t, j=j: e.activation(
                sq0[:], xs_t[:, sl, j], AF.Square, accum_out=ssq1[:, t:t + 1]),
                waits=[g("sXs")[sl]], incs=[g("cSTA")])
        if grp == NTILE // 8 - 1:
            STA_MID = g("cSTA").all()

    def stat_finalize(sumt, ssqt, nmut, rstt, n, nbt=None, lo=0, stat_waits=None):
        sw = [g("cSTA")] if stat_waits is None else stat_waits
        sl_ = slice(lo, n)
        B.op("vector", lambda e, sl_=sl_: e.tensor_scalar_mul(
            nmut[:, sl_], sumt[:, sl_], -1.0 / D), waits=sw + IDENT_OK)
        B.op("vector", lambda e, sl_=sl_: e.tensor_mul(
            stat_t[:, sl_], nmut[:, sl_], nmut[:, sl_]))
        B.op("vector", lambda e, sl_=sl_: e.scalar_tensor_tensor(
            stat_t[:, sl_], ssqt[:, sl_], 1.0 / D, stat_t[:, sl_],
            ALU.mult, ALU.subtract),
            incs=[g("cSF1")])
        B.op("scalar", lambda e, sl_=sl_: e.activation(
            rstt[:, sl_], stat_t[:, sl_], AF.Ln, bias=eps_t[:, 0:1]),
             waits=sw + [g("cSF1")] + IDENT_OK)
        B.op("scalar", lambda e, sl_=sl_: e.activation(
            rstt[:, sl_], rstt[:, sl_], AF.Exp, scale=-0.5), incs=[g("cRSD")])
        if nbt is not None:
            B.op("vector", lambda e, sl_=sl_: e.tensor_mul(
                nbt[:, sl_], nmut[:, sl_], rstt[:, sl_]),
                 waits=[g("cRSD")], incs=[g("cNB")])

    stat_finalize(sum1, ssq1, nmu1, rst1, NTILE // 2, nbt=nb1, stat_waits=STA_MID)
    stat_finalize(sum1, ssq1, nmu1, rst1, NTILE, nbt=nb1, lo=NTILE // 2)

    c_glob = [0]
    add_done = {}

    def ffn_pass(x_src, y_dst, nst, nmu, rst, nbt, ln2_sums, tY, first_gate_waits=(),
                 nb_wait=None):
        sXr, sWD = g("sXr"), g("sWD")
        cNRM, cTP, cTE, cGA, cUP = g("cNRM"), g("cTP"), g("cTE"), g("cGA"), g("cUP")
        cSIL, cH, cADD, cSTA = g("cSIL"), g("cH"), g("cADD"), g("cSTA")
        cDCm, cDWN = g("cDCm"), g("cDWN")
        for s in range(nst):
            for tau in range(4):
                t = s * 4 + tau
                ad = add_done.get((tY.name, t - 4))
                B.op("sync", lambda e, t=t, x=x_src: e.dma_start(
                    out=xn_t[:, t % 4], in_=x[t * 128:(t + 1) * 128, :]),
                    waits=([ad] if ad else []),
                    incs=[(sXr[t % 4], 16)])

            for tau in range(4):
                t = s * 4 + tau
                sl = tau % 2
                tt = t if nmu.shape[1] > 16 else t % 16
                B.op("scalar", lambda e, t=t, sl=sl, tt=tt, rst=rst, nbt=nbt: e.activation(
                    xn1tm[:, sl], xn_t[:, t % 4], AF.Identity,
                    scale=rst[:, tt:tt + 1], bias=nbt[:, tt:tt + 1]),
                    waits=[sXr[t % 4]]
                          + (nb_wait(t) if nb_wait else [g("cNB")])
                          + [(cTP, max(0, cTP.n - 1))],
                    incs=[cNRM])
                pt = pTv[sl]
                for fc in range(8):
                    B.op("tensor", lambda e, sl=sl, fc=fc, pt=pt: e.transpose(
                        pt[:].rearrange("p (f j) -> p f j", f=8)[:, fc],
                        xn1tm[:, sl, fc * 128:(fc + 1) * 128], ident[:]),
                        waits=(([cNRM, (cTE, 1)]
                                + (list(first_gate_waits) if (s == 0 and tau == 0) else []))
                               if fc == 0 else []),
                        incs=([cTP] if fc == 7 else []))
                B.op("scalar", lambda e, tau=tau, pt=pt: e.activation(
                    xn1fm[:].rearrange("p c (t j) -> p c t j", t=4)[:, :, tau],
                    pt[:].rearrange("p (f j) -> p f j", f=8), AF.Identity),
                    waits=[cTP, (cUP, cUP.n)], incs=[cTE])
            XN_T = cTE.back(0)
            for half in range(2):
                for grp in range(3):
                    cg = c_glob[0] + grp * 4
                    dc = cg % 24
                    slot = cg % 12
                    B.op("sync", lambda e, dc=dc, slot=slot: e.dma_start(
                        out=wdTr[:, slot:slot + 4].rearrange("p c f -> p (c f)"),
                        in_=wdT_dram[:, dc * D:(dc + 4) * D]),
                        waits=TWT_ALL + [cDCm],
                        incs=[(sWD[(cg // 4) % 3], 16)])
                for ci in range(12):
                    cn = half * 12 + ci
                    for fc in range(8):
                        B.op("tensor", lambda e, cn=cn, fc=fc: e.matmul(
                            pA[:], wgT[:, fc, cn * 128:(cn + 1) * 128], xn1fm[:, fc],
                            start=(fc == 0), stop=(fc == 7)),
                            waits=([XN_T, cSIL] if fc == 0 else []),
                            incs=([cGA] if fc == 7 else []))
                    for fc in range(8):
                        B.op("tensor", lambda e, cn=cn, fc=fc: e.matmul(
                            pB[:], wuT[:, fc, cn * 128:(cn + 1) * 128], xn1fm[:, fc],
                            start=(fc == 0), stop=(fc == 7)),
                            waits=([cH] if fc == 0 else []),
                            incs=([cUP] if fc == 7 else []))
                    B.op("scalar", lambda e, ci=ci: e.activation(
                        g_bf[:, ci % 2], pA[:], AF.Silu),
                        waits=[(cGA, cGA.n), (cH, 1)], incs=[cSIL])
                    B.op("vector", lambda e, ci=ci: e.tensor_tensor(
                        h_sb[:, ci], g_bf[:, ci % 2], pB[:], ALU.mult),
                        waits=[(cUP, cUP.n), cSIL, cDCm],
                        incs=[cH])
                for pair in range(2):
                    for tau, pd in ((pair * 2, pC), (pair * 2 + 1, pD)):
                        t = s * 4 + tau
                        for ci in range(12):
                            slot = (c_glob[0] + ci) % 12
                            for fh in range(2):
                                B.op("tensor", lambda e, ci=ci, tau=tau, fh=fh, pd=pd, slot=slot: e.matmul(
                                    pd[:, fh * 512:(fh + 1) * 512],
                                    h_sb[:, ci, tau * 128:(tau + 1) * 128],
                                    wdTr[:, slot, fh * 512:(fh + 1) * 512],
                                    start=(ci == 0), stop=(ci == 11)),
                                    waits=(([cH, cADD] + [(w, w.n) for w in sWD])
                                           if (ci == 0 and fh == 0) else []),
                                    incs=([cDWN] if (ci == 11 and fh == 1) else []))
                        if half == 0:
                            B.op("vector", lambda e, t=t, tau=tau, pd=pd: e.tensor_tensor(
                                x1a[:, tau], xn_t[:, t % 4], pd[:], ALU.add),
                                waits=[(tY, tY.n), (cDWN, cDWN.n)],
                                incs=[cADD] + ([cDCm] if tau == 3 else []))
                            add_done[(tY.name, t)] = cADD.back(0)
                        else:
                            B.op("vector", lambda e, tau=tau, pd=pd: e.tensor_tensor(
                                x1a[:, tau], x1a[:, tau], pd[:], ALU.add),
                                waits=[(cDWN, cDWN.n)],
                                incs=[cADD] + ([cDCm] if tau == 3 else []))
                            t_g = s * 4 + tau
                            if ln2_sums:
                                B.op("vector", lambda e, t=t_g, tau=tau: e.reduce_sum(
                                    sum2[:, t:t + 1], x1a[:, tau], axis=AX),
                                    waits=[cADD], incs=[cSTA])
                                B.op("scalar", lambda e, t=t_g, tau=tau: e.activation(
                                    sq1a[:], x1a[:, tau], AF.Square,
                                    accum_out=ssq2[:, t:t + 1]),
                                    waits=[cADD], incs=[cSTA])
                            B.op("gpsimd", lambda e, t=t_g, tau=tau, y=y_dst: e.dma_start(
                                out=y[t * 128:(t + 1) * 128, :], in_=x1a[:, tau]),
                                waits=[cADD] + ([cSTA] if ln2_sums else []),
                                incs=[(tY, 16)], dma=True)
                c_glob[0] += 12

    B.barrier()
    import os as _os
    nst_lim = int(_os.environ.get("NSTLIM", str(NST)))
    if upto >= 2:
        ffn_pass(x_in, x1_dram, nst_lim, nmu1, rst1, nb1, True, g("tX1"),
                 nb_wait=lambda t: [(g("cNB"), 1 if t < NTILE // 2 else 0)])

    B.barrier()
    if upto < 3:
        B.op("gpsimd", lambda e: e.engine_nop(),
             waits=[(g("tX1"), g("tX1").n)] + [(t, t.n) for t in g("tWT")])
        return _emit(nc, B, ctx)
    stat_finalize(sum2, ssq2, nmu2, rst2, NTILE)
    B.op("vector", lambda e: e.tensor_mul(sfold[:], rst2[:], rst2[:]),
         waits=[g("cRSD")])
    B.op("vector", lambda e: e.tensor_mul(sfold[:], stat_t[:], sfold[:]),
         incs=[g("cSF1")])
    B.op("scalar", lambda e: e.activation(sfold[:], sfold[:], AF.Ln, bias=eps_t[:, 0:1]),
         waits=[g("cSF1")])
    B.op("scalar", lambda e: e.activation(sfold[:], sfold[:], AF.Exp, scale=-0.5),
         incs=[g("cSF2")])
    B.op("vector", lambda e: e.tensor_mul(sfold[:], rst2[:], sfold[:]),
         waits=[g("cSF2")], incs=[g("cRSD")])
    B.op("vector", lambda e: e.tensor_mul(nbf[:], nmu2[:], sfold[:]), incs=[g("cNB")])

    for cn in range(8):
        B.op("gpsimd", lambda e, cn=cn: e.dma_start(
            out=win_sb[:, cn], in_=W_in[cn * 128:(cn + 1) * 128, :]),
            incs=[(g("sWI"), 16)], dma=True)
    WIN_LD = g("sWI").n

    cNRM, cTP, cTE = g("cNRM"), g("cTP"), g("cTE")
    cPE1, cVE, cKQ = g("cPE1"), g("cVE"), g("cKQ")
    nrm_done = {}
    xni_tau = {}
    bank_reader = {}

    def p1b_norm(sx):
        own = sx < NST_OWN
        for tau in range(4):
            t = sx * 4 + tau
            nd = nrm_done.get(t - 4) or []
            B.op("sync", lambda e, t=t: e.dma_start(
                out=x1r[:, t % 4], in_=x1_dram[t * 128:(t + 1) * 128, :]),
                waits=[(g("tX1"), g("tX1").n)] + list(nd),
                incs=[(g("sQ")[t % 4], 16)])
        for tau in range(4):
            t = sx * 4 + tau
            sl = tau % 2
            B.op("scalar", lambda e, t=t, sl=sl: e.activation(
                zn_tm[:, sl], x1r[:, t % 4], AF.Identity,
                scale=sfold[:, t:t + 1], bias=nbf[:, t:t + 1]),
                waits=[g("sQ")[t % 4], g("cNB"),
                       (cTP, max(0, cTP.n - 1))], incs=[cNRM])
            if own:
                B.op("vector", lambda e, t=t: e.tensor_scalar(
                    xm_tmp[:], x1r[:, t % 4], nmu2[:, t:t + 1], rst2[:, t:t + 1],
                    ALU.add, ALU.mult),
                    waits=[g("sQ")[t % 4], g("cRSD")])
                B.op("vector", lambda e, t=t: e.tensor_tensor(
                    xm_stg[:, t % 2], x1r[:, t % 4], xm_tmp[:], ALU.add),
                    waits=[g("tM")[t % 2]], incs=[cNRM])
                B.op("gpsimd", lambda e, t=t: e.dma_start(
                    out=xm_dram[t * 128:(t + 1) * 128, :], in_=xm_stg[:, t % 2]),
                    waits=[cNRM], incs=[(g("tM")[t % 2], 16)], dma=True)
            nrm_done[t] = cNRM.all()
            pt = pTv[sl]
            for fc in range(8):
                B.op("tensor", lambda e, sl=sl, fc=fc, pt=pt: e.transpose(
                    pt[:].rearrange("p (f j) -> p f j", f=8)[:, fc],
                    zn_tm[:, sl, fc * 128:(fc + 1) * 128], ident[:]),
                    waits=([cNRM, (cTE, 1)] if fc == 0 else []),
                    incs=([cTP] if fc == 7 else []))
            for fc in range(8):
                B.op("vector", lambda e, sx=sx, tau=tau, fc=fc, pt=pt: e.tensor_scalar(
                    xnifm[:, sx % 2, fc, tau * 128:(tau + 1) * 128],
                    pt[:].rearrange("p (f j) -> p f j", f=8)[:, fc],
                    lng_t[:, fc:fc + 1], lnb_t[:, fc:fc + 1], ALU.mult, ALU.add),
                    waits=([cTP, (cPE1, cPE1.n)] if fc == 0 else []),
                    incs=([cTE] if fc == 7 else []))
            xni_tau[(sx, tau)] = cTE.back(0)

    def p1b_mm(sx):
        own = sx < NST_OWN
        xb = sx % 2
        def brd(bank):
            r = bank_reader.get(id(bank))
            return [r] if r else []
        for tau in range(4):
            for eg in range(4):
                pv_ = pA if (tau * 4 + eg) % 2 == 0 else pB
                for fc in range(8):
                    B.op("tensor", lambda e, xb=xb, tau=tau, eg=eg, fc=fc, pv_=pv_: e.matmul(
                        pv_[:], xnifm[:, xb, fc, tau * 128:(tau + 1) * 128],
                        win_sb[:, fc, E + eg * 512:E + (eg + 1) * 512],
                        start=(fc == 0), stop=False),
                        waits=(([xni_tau[(sx, tau)], (g("sWI"), WIN_LD)]
                                + brd(pv_)) if fc == 0 else []))
                B.op("tensor", lambda e, eg=eg, pv_=pv_: e.matmul(
                    pv_[:], ones[0:1, 0:128],
                    binrow[0:1, E + eg * 512:E + (eg + 1) * 512],
                    start=False, stop=True), incs=[cPE1])
                B.op("scalar", lambda e, tau=tau, eg=eg, pv_=pv_: e.activation(
                    v_stg[:, tau % 2, eg * 512:(eg + 1) * 512], pv_[:], AF.Silu),
                    waits=[(cPE1, cPE1.n), g("tV")[tau % 2]],
                    incs=[cVE])
                bank_reader[id(pv_)] = cVE.back(0)
            B.op("gpsimd", lambda e, sx=sx, tau=tau: e.dma_start(
                out=v_dram[:, (sx * 4 + tau) * E:(sx * 4 + tau + 1) * E],
                in_=v_stg[:, tau % 2]), waits=[cVE], incs=[(g("tV")[tau % 2], 16)], dma=True)
        zsl = sx % 2
        for fc in range(8):
            B.op("tensor", lambda e, xb=xb, fc=fc: e.matmul(
                pA[:], win_sb[:, fc, 2 * E:2 * E + S], xnifm[:, xb, fc],
                start=(fc == 0), stop=(fc == 7)),
                waits=(brd(pA) if fc == 0 else []),
                incs=([cPE1] if fc == 7 else []))
        B.op("scalar", lambda e, zsl=zsl: e.activation(
            z_scr[:, zsl], pA[:], AF.Silu, bias=bz_t[:, 0:1]),
            waits=[(cPE1, cPE1.n), cKQ], incs=[cVE])
        bank_reader[id(pA)] = cVE.back(0)
        B.op("vector", lambda e, zsl=zsl: e.tensor_scalar(
            k_stg[:, zsl], z_scr[:, zsl], gk_t[:, 0:1], bk_t[:, 0:1],
            ALU.mult, ALU.add),
            waits=[cVE, g("tK")[zsl]], incs=[cKQ])
        B.op("gpsimd", lambda e, sx=sx, zsl=zsl: e.dma_start(
            out=k_dram[:, sx * 512:(sx + 1) * 512], in_=k_stg[:, zsl]),
            waits=[cKQ], incs=[(g("tK")[zsl], 16)], dma=True)
        if own:
            B.op("vector", lambda e, sx=sx, zsl=zsl: e.tensor_scalar(
                q_sb[:, sx * 512:(sx + 1) * 512], z_scr[:, zsl], gq_t[:, 0:1],
                bq_t[:, 0:1], ALU.mult, ALU.add), incs=[cKQ])
            for ecn in range(16):
                pu = pA if ecn % 2 == 0 else pB
                for fc in range(8):
                    B.op("tensor", lambda e, xb=xb, ecn=ecn, fc=fc, pu=pu: e.matmul(
                        pu[:], win_sb[:, fc, ecn * 128:(ecn + 1) * 128],
                        xnifm[:, xb, fc], start=(fc == 0), stop=(fc == 7)),
                        waits=(([cKQ] if ecn == 0 else []) + (brd(pu) if fc == 0 else [])),
                        incs=([cPE1] if fc == 7 else []))
                B.op("scalar", lambda e, ecn=ecn, pu=pu: e.activation(
                    u_stg[:, ecn % 4], pu[:], AF.Silu, bias=bu_t[:, ecn:ecn + 1]),
                    waits=[(cPE1, cPE1.n), g("tU")[ecn % 4]],
                    incs=[cVE])
                bank_reader[id(pu)] = cVE.back(0)
                B.op("gpsimd", lambda e, sx=sx, ecn=ecn: e.dma_start(
                    out=u_dram[ecn * 128:(ecn + 1) * 128, sx * 512:(sx + 1) * 512],
                    in_=u_stg[:, ecn % 4]), waits=[cVE], incs=[(g("tU")[ecn % 4], 16)], dma=True)

    p1b_norm(0)
    for sx in range(NST):
        if sx + 1 < NST:
            p1b_norm(sx + 1)
        p1b_mm(sx)

    if upto < 4:
        B.op("gpsimd", lambda e: e.engine_nop(),
             waits=[(t, t.n) for t in g("tV") + g("tK") + g("tU") + g("tM")])
        return _emit(nc, B, ctx)
    KB, NKB, CPB = 1024, 16, 8
    cSC, cRL, cP, cPV, cOE = g("cSC"), g("cRL"), g("cP"), g("cPV"), g("cOE")
    cOU, cY, cX2, cSTA = g("cOU"), g("cY"), g("cX2"), g("cSTA")

    for half in range(2):
        B.barrier()
        QOFF = half * 1024
        pdone = {}

        def emit_scores(kb, QOFF=None):
            base = (kb % 2) * CPB
            B.op("sync", lambda e, kb=kb, base=base: e.dma_start(
                out=vring[:, base:base + CPB].rearrange("p c e -> p (c e)"),
                in_=v_dram[:, kb * CPB * E:(kb + 1) * CPB * E]),
                waits=[(t, t.n) for t in g("tV")] + [(cPV, cPV.n)],
                incs=[(g("sV")[kb % 2], 16)])
            B.op("sync", lambda e, kb=kb: e.dma_start(
                out=kst[:, kb % 2], in_=k_dram[:, kb * KB:(kb + 1) * KB]),
                waits=[(t, t.n) for t in g("tK")] + [(cP, 16)],
                incs=[(g("sK")[kb % 2], 16)])
            psl = kb % 2
            for kt in range(CPB):
                for qs in range(2):
                    ps_ = pA if qs == 0 else pB
                    B.op("tensor", lambda e, kb=kb, kt=kt, qs=qs, ps_=ps_, QOFF=QOFF: e.matmul(
                        ps_[:], kst[:, kb % 2, kt * 128:(kt + 1) * 128],
                        q_sb[:, QOFF + qs * 512:QOFF + (qs + 1) * 512],
                        start=True, stop=True),
                        waits=([g("sK")[kb % 2], g("cKQ"), (cRL, 1)]
                               if (kt == 0 and qs == 0) else [(cRL, 1)]),
                        incs=[cSC])
                    rsl = (kt % 2) * 2 + qs
                    B.op("vector", lambda e, rsl=rsl, ps_=ps_: e.tensor_scalar_max(
                        rl_scr[:, rsl], ps_[:], 0.0),
                        waits=[(cSC, cSC.n), (cP, 3)], incs=[cRL])
                    B.op("scalar", lambda e, psl=psl, kt=kt, qs=qs, rsl=rsl: e.activation(
                        p_sb[:, psl, kt, qs * 512:(qs + 1) * 512], rl_scr[:, rsl],
                        AF.Square),
                        waits=[cRL, (cPV, cPV.n)], incs=[cP])
            pdone[kb] = cP.back(0)

        def emit_pv(kb):
            base = (kb % 2) * CPB
            psl = kb % 2
            pvslots = [pC[:, 0:512], pC[:, 512:1024], pD[:, 0:512], pD[:, 512:1024]]
            for ecn in range(16):
                for qs in range(2):
                    slot = pvslots[(ecn * 2 + qs) % 4]
                    for kt in range(CPB):
                        B.op("tensor", lambda e, base=base, ecn=ecn, qs=qs, kt=kt, slot=slot, psl=psl: e.matmul(
                            slot, vring[:, base + kt, ecn * 128:(ecn + 1) * 128],
                            p_sb[:, psl, kt, qs * 512:(qs + 1) * 512],
                            start=(kt == 0), stop=(kt == CPB - 1)),
                            waits=([pdone[kb], g("sV")[kb % 2],
                                    (cOE, 3)] if kt == 0 else []),
                            incs=([cPV] if kt == CPB - 1 else []))
                    if kb == 0:
                        B.op("vector", lambda e, ecn=ecn, qs=qs, slot=slot: e.tensor_copy(
                            o_sb[:, ecn, qs * 512:(qs + 1) * 512], slot),
                            waits=[(cPV, cPV.n)], incs=[cOE])
                    else:
                        B.op("vector", lambda e, ecn=ecn, qs=qs, slot=slot: e.tensor_tensor(
                            o_sb[:, ecn, qs * 512:(qs + 1) * 512],
                            o_sb[:, ecn, qs * 512:(qs + 1) * 512], slot, ALU.add),
                            waits=[(cPV, cPV.n)], incs=[cOE])

        emit_scores(0, QOFF=QOFF)
        for kb in range(NKB):
            if kb + 1 < NKB:
                emit_scores(kb + 1, QOFF=QOFF)
            emit_pv(kb)
        OE_ALL = cOE.back(0)
        for cn in range(16):
            B.op("gpsimd", lambda e, cn=cn: e.dma_start(
                out=wout_sb[:, cn], in_=W_out[cn * 128:(cn + 1) * 128, :]),
                waits=([(cPV, cPV.n - 32)] if cn == 0 else []),
                incs=[(g("sWO"), 16)], dma=True)
        WOUT_LD = g("sWO").n
        for ecn in range(16):
            B.op("sync", lambda e, ecn=ecn, QOFF=QOFF: e.dma_start(
                out=u_ld[:, ecn % 2],
                in_=u_dram[ecn * 128:(ecn + 1) * 128, QOFF:QOFF + 1024]),
                waits=[(t, t.n) for t in g("tU")] + [(cOU, 1), (cPV, cPV.n - 32)],
                incs=[(g("sU")[ecn % 2], 16)])
            B.op("vector", lambda e, ecn=ecn: e.tensor_tensor(
                ou_bf[:, ecn], u_ld[:, ecn % 2], o_sb[:, ecn], ALU.mult),
                waits=[g("sU")[ecn % 2], (cOE, 30 - 2 * ecn), (cPV, cPV.n)],
                incs=[cOU])
        OU_ALL = cOU.back(0)
        for qp in range(4):
            for tau, pd in ((0, pC), (1, pD)):
                qt = qp * 2 + tau
                tq = half * 8 + qt
                for ecn in range(16):
                    for fh in range(2):
                        B.op("tensor", lambda e, ecn=ecn, qt=qt, fh=fh, pd=pd: e.matmul(
                            pd[:, fh * 512:(fh + 1) * 512],
                            ou_bf[:, ecn, qt * 128:(qt + 1) * 128],
                            wout_sb[:, ecn, fh * 512:(fh + 1) * 512],
                            start=(ecn == 0), stop=False),
                            waits=(([(cOU, 15 - ecn)] if fh == 0 else [])
                                   + ([(g("sWO"), WOUT_LD), (cX2, 1)]
                                      if (ecn == 0 and fh == 0) else [])))
                for fh in range(2):
                    B.op("tensor", lambda e, fh=fh, pd=pd: e.matmul(
                        pd[:, fh * 512:(fh + 1) * 512], ones[0:1, 0:128],
                        bout_bf[0:1, fh * 512:(fh + 1) * 512], start=False, stop=True),
                        incs=([cY] if fh == 1 else []))
                B.op("sync", lambda e, tq=tq, qt=qt: e.dma_start(
                    out=xm_ld[:, qt % 2], in_=xm_dram[tq * 128:(tq + 1) * 128, :]),
                    waits=[(t, t.n) for t in g("tM")] + [(cX2, 1)],
                    incs=[(g("sM")[qt % 2], 16)])
                B.op("vector", lambda e, qt=qt, pd=pd: e.tensor_tensor(
                    x2_stg[:, qt % 2], xm_ld[:, qt % 2], pd[:], ALU.add),
                    waits=[g("sM")[qt % 2], (cY, cY.n),
                           g("tX2")[qt % 2]], incs=[cX2])
                B.op("vector", lambda e, tq=tq, qt=qt: e.reduce_sum(
                    sum3[:, tq:tq + 1], x2_stg[:, qt % 2], axis=AX), incs=[cSTA])
                B.op("scalar", lambda e, tq=tq, qt=qt: e.activation(
                    sq3_scr[:], x2_stg[:, qt % 2], AF.Square,
                    accum_out=ssq3[:, tq:tq + 1]), waits=[cX2], incs=[cSTA])
                B.op("gpsimd", lambda e, tq=tq, qt=qt: e.dma_start(
                    out=x2_dram[tq * 128:(tq + 1) * 128, :], in_=x2_stg[:, qt % 2]),
                    waits=[cX2, cSTA], incs=[(g("tX2")[qt % 2], 16)], dma=True)

    B.barrier()
    if upto < 5:
        B.op("gpsimd", lambda e: e.engine_nop(),
             waits=[(t, t.n) for t in g("tX2")])
        return _emit(nc, B, ctx)
    B.op("sync", lambda e: e.dma_start(
        out=wgT[:].rearrange("p c h -> p (c h)"), in_=wgT_dram),
        incs=[(g("sR"), 16)])
    B.op("sync", lambda e: e.dma_start(
        out=wuT[:].rearrange("p c h -> p (c h)"), in_=wuT_dram),
        incs=[(g("sR"), 16)])
    W_RELOAD = g("sR").n
    stat_finalize(sum3, ssq3, nmu3, rst3, 16, nbt=nb3)
    ffn_pass(x2_dram, out, NST_OWN, nmu3, rst3, nb3, False, g("tOUT"),
             first_gate_waits=[(g("sR"), W_RELOAD)])

    B.op("gpsimd", lambda e: e.engine_nop(), waits=[(g("tOUT"), g("tOUT").n)])
    return _emit(nc, B, ctx)


def _emit(nc, B, ctx):
    with nc.Block() as blk:
        @blk.sync
        def _(e):
            for f in B.streams["sync"]:
                f(e)

        @blk.gpsimd
        def _(e):
            for f in B.streams["gpsimd"]:
                f(e)

        @blk.tensor
        def _(e):
            for f in B.streams["tensor"]:
                f(e)

        @blk.vector
        def _(e):
            for f in B.streams["vector"]:
                f(e)

        @blk.scalar
        def _(e):
            for f in B.streams["scalar"]:
                f(e)

    ctx.close()
    return nc


def kernel(**inputs):
    x = np.asarray(inputs["L0_attn_x"]).reshape(T_ALL, D)
    nc = build_nc(dbg=False)
    in_maps = []
    for c in range(NC_N):
        m = {k: np.ascontiguousarray(np.asarray(v)) for k, v in inputs.items()
             if k != "L0_attn_x"}
        m["L0_attn_x"] = np.ascontiguousarray(np.roll(x, -T_OWN * c, axis=0))
        in_maps.append(m)
    res = run_bass_kernel_spmd(nc, in_maps, core_ids=list(range(NC_N)))
    o = np.empty((T_ALL, D), dtype=np.float32)
    for c in range(NC_N):
        o[c * T_OWN:(c + 1) * T_OWN] = res.results[c]["out"]
    return o.reshape(1, T_ALL, D)


# version 07
# speedup vs baseline: 1.0447x; 1.0447x over previous
import numpy as np
import concourse.bass as bass
import concourse.mybir as mybir
from concourse.bass_utils import run_bass_kernel_spmd
from contextlib import ExitStack

F32 = mybir.dt.float32
BF16 = mybir.dt.bfloat16
I32 = mybir.dt.int32
AF = mybir.ActivationFunctionType
ALU = mybir.AluOpType
AX = mybir.AxisListType.X

T_ALL, T_OWN, D, H, E, S, FUVZ = 16384, 2048, 1024, 3072, 2048, 128, 4224
LSEQ = 16384.0
EPS = 1e-5
NC_N = 8
ST = 512
NST = T_ALL // ST
NST_OWN = T_OWN // ST
NTILE = T_ALL // 128
DQ = 1.0 / 127.0
SB_BASE = 16512
SB_LIMIT = 229376


class Sem:
    __slots__ = ("h", "n", "name")


class Mile:
    __slots__ = ("name", "hist", "targets")

    def __init__(self, name):
        self.name = name
        self.hist = []
        self.targets = {}

    def rec(self, h, n):
        self.hist.append((h, n))
        self.targets[id(h)] = (h, n)

    def back(self, k):
        return self.hist[-1 - k] if len(self.hist) > k else None

    def all(self):
        return list(self.targets.values())


class Builder:
    CHAINED = ("vector", "scalar", "gpsimd")

    def __init__(self, nc, ctx):
        self.nc, self.ctx = nc, ctx
        self.streams = {k: [] for k in ("sync", "gpsimd", "tensor", "vector", "scalar")}
        self.pending = {k: [] for k in self.streams}
        self.all_sems = []
        self.chains = {}

    def sem(self, name):
        s = Sem()
        s.h = self.ctx.enter_context(self.nc.semaphore(name))
        s.n = 0
        s.name = name
        self.all_sems.append(s)
        return s

    def snap(self):
        return [(s, s.n) for s in self.all_sems if s.n > 0]

    def barrier(self, snap=None):
        snap = self.snap() if snap is None else snap
        for k in self.pending:
            self.pending[k] = list(snap)

    def op(self, eng, builder, waits=(), incs=(), dma=False):
        wl = {}

        def add(h, v):
            if v > 0:
                wl[id(h)] = (h, max(wl[id(h)][1], v) if id(h) in wl else v)

        for w in list(self.pending[eng]) + list(waits):
            if isinstance(w, Sem):
                add(w.h, w.n)
            elif isinstance(w, Mile):
                for h, v in w.all():
                    add(h, v)
            elif isinstance(w, tuple) and isinstance(w[0], Sem):
                add(w[0].h, w[1])
            elif isinstance(w, tuple) and isinstance(w[0], Mile):
                b = w[0].back(w[1])
                if b is not None:
                    add(b[0], b[1])
            elif isinstance(w, tuple) and len(w) == 2:
                add(w[0], w[1])
            else:
                raise TypeError(f"bad wait {w}")
        self.pending[eng] = []

        chained = eng in self.chains and not dma
        il = []
        if chained:
            ch = self.chains[eng]
            if ch.n > 0:
                add(ch.h, ch.n)
            ch.n += 1
            il.append((ch.h, 1))
            for i in incs:
                m = i if not isinstance(i, tuple) else i[0]
                assert isinstance(m, Mile), f"chained op can only inc Mile, got {m}"
                m.rec(ch.h, ch.n)
        else:
            for i in incs:
                s2, k = i if isinstance(i, tuple) else (i, 1)
                assert isinstance(s2, Sem), f"unchained op needs Sem inc, got {s2}"
                s2.n += k
                il.append((s2.h, k))
            assert len(il) <= 1, "one sem update per instruction"
        wl = list(wl.values())

        def f(e, b=builder, wl=wl, il=il):
            for h, v in wl:
                e.wait_ge(h, v)
            ins = b(e)
            for h, k in il:
                ins.then_inc(h, k)
        self.streams[eng].append(f)


def build_nc(dbg=False, upto=5):
    nc = bass.Bass("TRN2", target_bir_lowering=False, debug=False)
    ctx = ExitStack()

    x_in = nc.dram_tensor("L0_attn_x", [T_ALL, D], F32, kind="ExternalInput").ap()
    ln_g = nc.dram_tensor("L0_attn_ln_g", [D], F32, kind="ExternalInput").ap()
    ln_b = nc.dram_tensor("L0_attn_ln_b", [D], F32, kind="ExternalInput").ap()
    W_in = nc.dram_tensor("L0_attn_W_in", [D, FUVZ], F32, kind="ExternalInput").ap()
    b_in = nc.dram_tensor("L0_attn_b_in", [FUVZ], F32, kind="ExternalInput").ap()
    W_out = nc.dram_tensor("L0_attn_W_out", [E, D], F32, kind="ExternalInput").ap()
    b_out = nc.dram_tensor("L0_attn_b_out", [D], F32, kind="ExternalInput").ap()
    gam_q = nc.dram_tensor("L0_attn_gamma_q", [S], F32, kind="ExternalInput").ap()
    bet_q = nc.dram_tensor("L0_attn_beta_q", [S], F32, kind="ExternalInput").ap()
    gam_k = nc.dram_tensor("L0_attn_gamma_k", [S], F32, kind="ExternalInput").ap()
    bet_k = nc.dram_tensor("L0_attn_beta_k", [S], F32, kind="ExternalInput").ap()
    wg_i = nc.dram_tensor("L0_ffn_w_gate", [H, D], I32, kind="ExternalInput").ap()
    wu_i = nc.dram_tensor("L0_ffn_w_up", [H, D], I32, kind="ExternalInput").ap()
    wd_i = nc.dram_tensor("L0_ffn_w_down", [D, H], I32, kind="ExternalInput").ap()
    out = nc.dram_tensor("out", [T_OWN, D], F32, kind="ExternalOutput").ap()

    def dscr(name, shape, dtype, as_out=False):
        kind = "ExternalOutput" if (dbg and as_out) else "Internal"
        return nc.dram_tensor(name, shape, dtype, kind=kind).ap()

    x1_dram = dscr("dbg_x1", [T_ALL, D], F32, as_out=True)
    v_dram = dscr("dbg_v", [128, NTILE * E], BF16, as_out=True)
    k_dram = dscr("dbg_k", [S, T_ALL], BF16, as_out=True)
    u_dram = dscr("dbg_u", [E, T_OWN], BF16, as_out=True)
    xm_dram = dscr("dbg_xm", [T_OWN, D], F32, as_out=True)
    x2_dram = dscr("dbg_x2", [T_OWN, D], F32, as_out=True)
    wgT_dram = dscr("wgT_dram", [128, 8 * H], BF16)
    wuT_dram = dscr("wuT_dram", [128, 8 * H], BF16)
    wdT_dram = dscr("wdT_dram", [128, 24 * D], BF16)

    class Cur:
        def __init__(self, base):
            self.off = base

        def al(self, name, shape, dtype):
            nbytes = int(np.prod(shape[1:])) * mybir.dt.size(dtype)
            o = self.off
            self.off = (self.off + nbytes + 31) & ~31
            assert self.off <= SB_LIMIT, f"SBUF overflow at {name}: {self.off}"
            return nc.alloc_sbuf_tensor_at(name, list(shape), dtype, offset=o).ap()

    cc = Cur(SB_BASE)
    ident = cc.al("ident", [128, 128], BF16)
    ones = cc.al("ones", [1, 512], BF16)
    bout_bf = cc.al("bout_bf", [1, D], BF16)
    bz_t = cc.al("bz", [128, 1], F32)
    bu_t = cc.al("bu", [128, 16], F32)
    gq_t = cc.al("gq", [128, 1], F32)
    bq_t = cc.al("bq", [128, 1], F32)
    gk_t = cc.al("gk", [128, 1], F32)
    bk_t = cc.al("bk", [128, 1], F32)
    lng_t = cc.al("lng", [128, 8], F32)
    lnb_t = cc.al("lnb", [128, 8], F32)
    eps_t = cc.al("eps", [128, 1], F32)
    sum1 = cc.al("sum1", [128, NTILE], F32)
    ssq1 = cc.al("ssq1", [128, NTILE], F32)
    nmu1 = cc.al("nmu1", [128, NTILE], F32)
    rst1 = cc.al("rst1", [128, NTILE], F32)
    sum2 = cc.al("sum2", [128, NTILE], F32)
    ssq2 = cc.al("ssq2", [128, NTILE], F32)
    nmu2 = cc.al("nmu2", [128, NTILE], F32)
    rst2 = cc.al("rst2", [128, NTILE], F32)
    sfold = cc.al("sfold", [128, NTILE], F32)
    stat_t = cc.al("stat_t", [128, NTILE], F32)
    sum3 = cc.al("sum3", [128, 16], F32)
    ssq3 = cc.al("ssq3", [128, 16], F32)
    nmu3 = cc.al("nmu3", [128, 16], F32)
    rst3 = cc.al("rst3", [128, 16], F32)
    nb1 = cc.al("nb1", [128, NTILE], F32)
    nbf = cc.al("nbf", [128, NTILE], F32)
    nb3 = cc.al("nb3", [128, 16], F32)
    q_sb = cc.al("q_sb", [128, T_OWN], BF16)
    PB = (cc.off + 63) & ~63

    ca = Cur(PB)
    wgT = ca.al("wgT", [128, 8, H], BF16)
    wuT = ca.al("wuT", [128, 8, H], BF16)
    SCR = ca.off
    c = Cur(SCR)
    wd_bf = c.al("wd_bf", [128, 8, H // 4], BF16)
    wint = c.al("wint", [128, 2, 3, D], I32)
    wbf = c.al("wbf", [128, 2, 3, D], BF16)
    rowst = c.al("rowst", [128, 2, 8, 128], BF16)
    xs_t = c.al("xs_t", [128, 2, 4, D], F32)
    sq0 = c.al("sq0", [128, D], F32)
    c = Cur(SCR)
    wdTr = c.al("wdTr", [128, 12, D], BF16)
    h_sb = c.al("h_sb", [128, 12, ST], BF16)
    g_bf = c.al("g_bf", [128, 2, ST], BF16)
    xn_t = c.al("xn_t", [128, 4, D], F32)
    x1a = c.al("x1a", [128, 4, D], F32)
    xn1tm = c.al("xn1tm", [128, 2, D], BF16)
    xn1fm = c.al("xn1fm", [128, 2, 8, ST], BF16)
    sq1a = c.al("sq1a", [128, D], F32)
    c = Cur(PB)
    win_sb = c.al("win_sb", [128, 8, FUVZ], BF16)
    binrow = c.al("binrow", [1, FUVZ], BF16)
    x1r = c.al("x1r", [128, 4, D], F32)
    zn_tm = c.al("zn_tm", [128, 2, D], BF16)
    xnifm = c.al("xnifm", [128, 2, 8, ST], BF16)
    v_stg = c.al("v_stg", [128, 2, E], BF16)
    z_scr = c.al("z_scr", [128, 2, ST], F32)
    k_stg = c.al("k_stg", [128, 2, ST], BF16)
    u_stg = c.al("u_stg", [128, 4, ST], BF16)
    xm_stg = c.al("xm_stg", [128, 2, D], F32)
    xm_tmp = c.al("xm_tmp", [128, D], F32)
    c = Cur(PB)
    o_sb = c.al("o_sb", [128, 16, 1024], F32)
    ALIAS = c.off
    vring = c.al("vring", [128, 16, E], BF16)
    p_sb = c.al("p_sb", [128, 2, 8, 1024], BF16)
    rl_scr = c.al("rl_scr", [128, 4, 512], F32)
    kst = c.al("kst", [128, 2, 1024], BF16)
    c = Cur(ALIAS)
    wout_sb = c.al("wout_sb", [128, 16, 1024], BF16)
    ou_bf = c.al("ou_bf", [128, 16, 1024], BF16)
    u_ld = c.al("u_ld", [128, 2, 1024], BF16)
    xm_ld = c.al("xm_ld", [128, 2, D], F32)
    x2_stg = c.al("x2_stg", [128, 2, D], F32)
    sq3_scr = c.al("sq3", [128, D], F32)

    pA = ctx.enter_context(nc.psum_tensor("pA", [128, 512], F32))
    pB = ctx.enter_context(nc.psum_tensor("pB", [128, 512], F32))
    pC = ctx.enter_context(nc.psum_tensor("pC", [128, 1024], F32))
    pD = ctx.enter_context(nc.psum_tensor("pD", [128, 1024], F32))
    pT = ctx.enter_context(nc.psum_tensor("pT", [128, 1024], BF16))
    pT2 = ctx.enter_context(nc.psum_tensor("pT2", [128, 1024], BF16))
    pTv = [pT, pT2]

    B = Builder(nc, ctx)
    B.chains = {k: B.sem("chain_" + k) for k in ("vector", "scalar", "gpsimd")}
    sm = {n: B.sem(n) for n in (
        "sC", "sCB", "sWI", "sWO", "sR",
        "tX1", "tOUT",
        "cTP", "cGA", "cUP", "cDWN", "cPE1", "cSC", "cPV", "cY")}
    for base, k in (("sW", 2), ("sXs", 2), ("sXr", 4), ("sQ", 4), ("sWD", 3),
                    ("sV", 2), ("sK", 2), ("sU", 2), ("sM", 2),
                    ("tV", 2), ("tK", 2), ("tU", 4), ("tM", 2), ("tX2", 2),
                    ("tWT", 3)):
        sm[base] = [B.sem(f"{base}{i}") for i in range(k)]
    for n in ("cDEQ", "cTE", "cNRM", "cSIL", "cH", "cADD", "cSTA", "cRSD",
              "cVE", "cKQ", "cRL", "cP", "cOE", "cOU", "cX2", "cMISC",
              "cSF1", "cSF2", "cDCm", "cNB"):
        sm[n] = Mile(n)
    g = lambda n: sm[n]

    def ld_const(dst, src):
        B.op("gpsimd", lambda e, d=dst, s=src: e.dma_start(out=d, in_=s),
             incs=[(g("sC"), 16)], dma=True)

    ld_const(bz_t[:, 0], b_in[2 * E:2 * E + S])
    for ecn in range(16):
        ld_const(bu_t[:, ecn], b_in[ecn * 128:(ecn + 1) * 128])
    ld_const(gq_t[:, 0], gam_q)
    ld_const(bq_t[:, 0], bet_q)
    ld_const(gk_t[:, 0], gam_k)
    ld_const(bk_t[:, 0], bet_k)
    for cn in range(8):
        ld_const(lng_t[:, cn], ln_g[cn * 128:(cn + 1) * 128])
        ld_const(lnb_t[:, cn], ln_b[cn * 128:(cn + 1) * 128])
    B.op("gpsimd", lambda e: e.dma_start(
        out=bout_bf[0:1, :], in_=b_out.rearrange("(a f) -> a f", a=1)),
        incs=[(g("sCB"), 16)], dma=True)
    NCONST = g("sC").n

    B.op("gpsimd", lambda e: e.memset(ident[:], 0.0))
    B.op("gpsimd", lambda e: e.affine_select(
        out=ident[:], in_=ident[:], compare_op=ALU.not_equal, fill=1.0,
        base=0, pattern=[[-1, 128]], channel_multiplier=1), incs=[g("cMISC")])
    B.op("gpsimd", lambda e: e.memset(ones[:], 1.0), incs=[g("cMISC")])
    B.op("vector", lambda e: e.memset(eps_t[:], EPS), incs=[g("cMISC")])
    B.op("scalar", lambda e: e.mul(gq_t[:], gq_t[:], 1.0 / LSEQ),
         waits=[(g("sC"), NCONST)], incs=[g("cMISC")])
    B.op("scalar", lambda e: e.mul(bq_t[:], bq_t[:], 1.0 / LSEQ), incs=[g("cMISC")])
    IDENT_OK = g("cMISC").all()

    def prep_gu(w_src, wT_dst):
        for grp in range(8):
            sl = grp % 2
            B.op("gpsimd", lambda e, grp=grp, sl=sl, w=w_src: e.dma_start(
                out=wint[:, sl],
                in_=w[grp * 384:(grp + 1) * 384, :].rearrange("(c p) f -> p c f", p=128)),
                waits=[(g("cDEQ"), 1)], incs=[(g("sW")[sl], 16)], dma=True)
            B.op("vector", lambda e, sl=sl: e.tensor_scalar_mul(
                wbf[:, sl], wint[:, sl], DQ),
                waits=[g("sW")[sl], (g("cTP"), max(0, g("cTP").n - 2))],
                incs=[g("cDEQ")])
            for j in range(3):
                r = grp * 3 + j
                pt = pTv[r % 2]
                for fc in range(8):
                    B.op("tensor", lambda e, sl=sl, j=j, fc=fc, pt=pt: e.transpose(
                        pt[:].rearrange("p (f j2) -> p f j2", f=8)[:, fc],
                        wbf[:, sl, j, fc * 128:(fc + 1) * 128], ident[:]),
                        waits=(IDENT_OK + [g("cDEQ"), (g("cTE"), 1)] if fc == 0 else []),
                        incs=([g("cTP")] if fc == 7 else []))
                B.op("vector", lambda e, r=r, pt=pt, wT=wT_dst: e.tensor_copy(
                    wT[:, :, r * 128:(r + 1) * 128],
                    pt[:].rearrange("p (f j2) -> p f j2", f=8)),
                    waits=[g("cTP")], incs=[g("cTE")])

    prep_gu(wg_i, wgT)
    prep_gu(wu_i, wuT)
    B.op("gpsimd", lambda e: e.dma_start(
        out=wgT_dram, in_=wgT[:].rearrange("p c h -> p (c h)")),
        waits=[g("cTE")], incs=[(g("tWT")[2], 16)], dma=True)
    B.op("gpsimd", lambda e: e.dma_start(
        out=wuT_dram, in_=wuT[:].rearrange("p c h -> p (c h)")),
        incs=[(g("tWT")[2], 16)], dma=True)

    for ch in range(4):
        tp_guard = (g("cTP"), g("cTP").n)
        for r in range(8):
            sl = r % 2
            B.op("gpsimd", lambda e, r=r, sl=sl, ch=ch: e.dma_start(
                out=wint[:, sl].rearrange("p c f -> p (c f)")[:, 0:768],
                in_=wd_i[r * 128:(r + 1) * 128, ch * 768:(ch + 1) * 768]),
                waits=[(g("cDEQ"), 1)], incs=[(g("sW")[sl], 16)], dma=True)
            B.op("vector", lambda e, r=r, sl=sl: e.tensor_scalar_mul(
                wd_bf[:, r, :], wint[:, sl].rearrange("p c f -> p (c f)")[:, 0:768],
                0.5 * DQ),
                waits=[g("sW")[sl], tp_guard], incs=[g("cDEQ")])
        WD_DEQ = g("cDEQ").back(0)
        for ci in range(6):
            cn = ch * 6 + ci
            sl = cn % 2
            pt = pTv[sl]
            for r in range(8):
                B.op("tensor", lambda e, ci=ci, r=r, pt=pt: e.transpose(
                    pt[:].rearrange("p (f j) -> p f j", f=8)[:, r],
                    wd_bf[:, r, ci * 128:(ci + 1) * 128], ident[:]),
                    waits=([WD_DEQ, (g("cTE"), 1)] if r == 0 else []),
                    incs=([g("cTP")] if r == 7 else []))
            B.op("vector", lambda e, sl=sl, pt=pt: e.tensor_copy(
                rowst[:, sl], pt[:].rearrange("p (f j) -> p f j", f=8)),
                waits=[g("cTP"), g("tWT")[sl]], incs=[g("cTE")])
            B.op("gpsimd", lambda e, cn=cn, sl=sl: e.dma_start(
                out=wdT_dram[:, cn * D:(cn + 1) * D],
                in_=rowst[:, sl].rearrange("p r j -> p (r j)")),
                waits=[g("cTE")], incs=[(g("tWT")[sl], 16)], dma=True)
    TWT_ALL = [(t, t.n) for t in g("tWT")]

    STA_MID = None
    for grp in range(NTILE // 4):
        sl = grp % 2
        B.op("sync", lambda e, grp=grp, sl=sl: e.dma_start(
            out=xs_t[:, sl],
            in_=x_in[grp * 512:(grp + 1) * 512, :].rearrange("(c p) f -> p c f", p=128)),
            waits=[(g("cSTA"), 8), (g("cSTA"), 9)], incs=[(g("sXs")[sl], 16)])
        for j in range(4):
            t = grp * 4 + j
            B.op("vector", lambda e, sl=sl, t=t, j=j: e.reduce_sum(
                sum1[:, t:t + 1], xs_t[:, sl, j], axis=AX), waits=[g("sXs")[sl]],
                incs=[g("cSTA")])
            B.op("scalar", lambda e, sl=sl, t=t, j=j: e.activation(
                sq0[:], xs_t[:, sl, j], AF.Square, accum_out=ssq1[:, t:t + 1]),
                waits=[g("sXs")[sl]], incs=[g("cSTA")])
        if grp == NTILE // 8 - 1:
            STA_MID = g("cSTA").all()

    def stat_finalize(sumt, ssqt, nmut, rstt, n, nbt=None, lo=0, stat_waits=None):
        sw = [g("cSTA")] if stat_waits is None else stat_waits
        sl_ = slice(lo, n)
        B.op("vector", lambda e, sl_=sl_: e.tensor_scalar_mul(
            nmut[:, sl_], sumt[:, sl_], -1.0 / D), waits=sw + IDENT_OK)
        B.op("vector", lambda e, sl_=sl_: e.tensor_mul(
            stat_t[:, sl_], nmut[:, sl_], nmut[:, sl_]))
        B.op("vector", lambda e, sl_=sl_: e.scalar_tensor_tensor(
            stat_t[:, sl_], ssqt[:, sl_], 1.0 / D, stat_t[:, sl_],
            ALU.mult, ALU.subtract),
            incs=[g("cSF1")])
        B.op("scalar", lambda e, sl_=sl_: e.activation(
            rstt[:, sl_], stat_t[:, sl_], AF.Ln, bias=eps_t[:, 0:1]),
             waits=sw + [g("cSF1")] + IDENT_OK)
        B.op("scalar", lambda e, sl_=sl_: e.activation(
            rstt[:, sl_], rstt[:, sl_], AF.Exp, scale=-0.5), incs=[g("cRSD")])
        if nbt is not None:
            B.op("vector", lambda e, sl_=sl_: e.tensor_mul(
                nbt[:, sl_], nmut[:, sl_], rstt[:, sl_]),
                 waits=[g("cRSD")], incs=[g("cNB")])

    stat_finalize(sum1, ssq1, nmu1, rst1, NTILE // 2, nbt=nb1, stat_waits=STA_MID)
    stat_finalize(sum1, ssq1, nmu1, rst1, NTILE, nbt=nb1, lo=NTILE // 2)

    c_glob = [0]
    add_done = {}

    def ffn_pass(x_src, y_dst, nst, nmu, rst, nbt, ln2_sums, tY, first_gate_waits=(),
                 nb_wait=None):
        sXr, sWD = g("sXr"), g("sWD")
        cNRM, cTP, cTE, cGA, cUP = g("cNRM"), g("cTP"), g("cTE"), g("cGA"), g("cUP")
        cSIL, cH, cADD, cSTA = g("cSIL"), g("cH"), g("cADD"), g("cSTA")
        cDCm, cDWN = g("cDCm"), g("cDWN")
        pd_rd = {}

        def norm_block(s2, first=False):
            for tau in range(4):
                t = s2 * 4 + tau
                ad = add_done.get((tY.name, t - 4))
                B.op("sync", lambda e, t=t, x=x_src: e.dma_start(
                    out=xn_t[:, t % 4], in_=x[t * 128:(t + 1) * 128, :]),
                    waits=([ad] if ad else []),
                    incs=[(sXr[t % 4], 16)])
            for tau in range(4):
                t = s2 * 4 + tau
                sl = tau % 2
                tt = t if nmu.shape[1] > 16 else t % 16
                B.op("scalar", lambda e, t=t, sl=sl, tt=tt, rst=rst, nbt=nbt: e.activation(
                    xn1tm[:, sl], xn_t[:, t % 4], AF.Identity,
                    scale=rst[:, tt:tt + 1], bias=nbt[:, tt:tt + 1]),
                    waits=[sXr[t % 4]]
                          + (nb_wait(t) if nb_wait else [g("cNB")])
                          + [(cTP, max(0, cTP.n - 1))],
                    incs=[cNRM])
                pt = pTv[sl]
                for fc in range(8):
                    B.op("tensor", lambda e, sl=sl, fc=fc, pt=pt: e.transpose(
                        pt[:].rearrange("p (f j) -> p f j", f=8)[:, fc],
                        xn1tm[:, sl, fc * 128:(fc + 1) * 128], ident[:]),
                        waits=(([cNRM, (cTE, 1)]
                                + (list(first_gate_waits) if (first and tau == 0) else []))
                               if fc == 0 else []),
                        incs=([cTP] if fc == 7 else []))
                B.op("vector", lambda e, s2=s2, tau=tau, pt=pt: e.tensor_copy(
                    xn1fm[:, s2 % 2].rearrange("p c (t j) -> p c t j", t=4)[:, :, tau],
                    pt[:].rearrange("p (f j) -> p f j", f=8)),
                    waits=[cTP, (cUP, cUP.n)], incs=[cTE])

        def half_block(s2, half):
            xb = s2 % 2
            for grp in range(3):
                cg = c_glob[0] + grp * 4
                dc = cg % 24
                slot = cg % 12
                B.op("sync", lambda e, dc=dc, slot=slot: e.dma_start(
                    out=wdTr[:, slot:slot + 4].rearrange("p c f -> p (c f)"),
                    in_=wdT_dram[:, dc * D:(dc + 4) * D]),
                    waits=TWT_ALL + [cDCm],
                    incs=[(sWD[(cg // 4) % 3], 16)])
            for ci in range(12):
                cn = half * 12 + ci
                for fc in range(8):
                    B.op("tensor", lambda e, xb=xb, cn=cn, fc=fc: e.matmul(
                        pA[:], wgT[:, fc, cn * 128:(cn + 1) * 128], xn1fm[:, xb, fc],
                        start=(fc == 0), stop=(fc == 7)),
                        waits=([XN_T[s2], cSIL] if fc == 0 else []),
                        incs=([cGA] if fc == 7 else []))
                for fc in range(8):
                    B.op("tensor", lambda e, xb=xb, cn=cn, fc=fc: e.matmul(
                        pB[:], wuT[:, fc, cn * 128:(cn + 1) * 128], xn1fm[:, xb, fc],
                        start=(fc == 0), stop=(fc == 7)),
                        waits=([cH] if fc == 0 else []),
                        incs=([cUP] if fc == 7 else []))
                B.op("scalar", lambda e, ci=ci: e.activation(
                    g_bf[:, ci % 2], pA[:], AF.Silu),
                    waits=[(cGA, cGA.n), (cH, 1)], incs=[cSIL])
                B.op("vector", lambda e, ci=ci: e.tensor_tensor(
                    h_sb[:, ci], g_bf[:, ci % 2], pB[:], ALU.mult),
                    waits=[(cUP, cUP.n), cSIL, cDCm],
                    incs=[cH])
            for pair in range(2):
                for tau, pd in ((pair * 2, pC), (pair * 2 + 1, pD)):
                    t = s2 * 4 + tau
                    for ci in range(12):
                        slot = (c_glob[0] + ci) % 12
                        for fh in range(2):
                            B.op("tensor", lambda e, ci=ci, tau=tau, fh=fh, pd=pd, slot=slot: e.matmul(
                                pd[:, fh * 512:(fh + 1) * 512],
                                h_sb[:, ci, tau * 128:(tau + 1) * 128],
                                wdTr[:, slot, fh * 512:(fh + 1) * 512],
                                start=(ci == 0), stop=(ci == 11)),
                                waits=(([cH] + [(w, w.n) for w in sWD]
                                        + (pd_rd.get(id(pd)) or []))
                                       if (ci == 0 and fh == 0) else []),
                                incs=([cDWN] if (ci == 11 and fh == 1) else []))
                    if half == 0:
                        B.op("vector", lambda e, t=t, tau=tau, pd=pd: e.tensor_tensor(
                            x1a[:, tau], xn_t[:, t % 4], pd[:], ALU.add),
                            waits=[(tY, tY.n), (cDWN, cDWN.n)],
                            incs=[cADD] + ([cDCm] if tau == 3 else []))
                        add_done[(tY.name, t)] = cADD.back(0)
                        pd_rd[id(pd)] = [cADD.back(0)]
                    else:
                        B.op("vector", lambda e, tau=tau, pd=pd: e.tensor_tensor(
                            x1a[:, tau], x1a[:, tau], pd[:], ALU.add),
                            waits=[(cDWN, cDWN.n)],
                            incs=[cADD] + ([cDCm] if tau == 3 else []))
                        pd_rd[id(pd)] = [cADD.back(0)]
                        t_g = s2 * 4 + tau
                        if ln2_sums:
                            B.op("scalar", lambda e, t=t_g, tau=tau: e.activation(
                                sq1a[:], x1a[:, tau], AF.Identity,
                                accum_out=sum2[:, t:t + 1]),
                                waits=[cADD], incs=[cSTA])
                            B.op("scalar", lambda e, t=t_g, tau=tau: e.activation(
                                sq1a[:], x1a[:, tau], AF.Square,
                                accum_out=ssq2[:, t:t + 1]),
                                waits=[cADD], incs=[cSTA])
                        B.op("gpsimd", lambda e, t=t_g, tau=tau, y=y_dst: e.dma_start(
                            out=y[t * 128:(t + 1) * 128, :], in_=x1a[:, tau]),
                            waits=[cADD] + ([cSTA] if ln2_sums else []),
                            incs=[(tY, 16)], dma=True)
            c_glob[0] += 12

        XN_T = {}
        norm_block(0, first=True)
        XN_T[0] = cTE.back(0)
        for s2 in range(nst):
            half_block(s2, 0)
            if s2 + 1 < nst:
                norm_block(s2 + 1)
                XN_T[s2 + 1] = cTE.back(0)
            half_block(s2, 1)

    B.barrier()
    import os as _os
    nst_lim = int(_os.environ.get("NSTLIM", str(NST)))
    if upto >= 2:
        ffn_pass(x_in, x1_dram, nst_lim, nmu1, rst1, nb1, True, g("tX1"),
                 nb_wait=lambda t: [(g("cNB"), 1 if t < NTILE // 2 else 0)])

    B.barrier()
    if upto < 3:
        B.op("gpsimd", lambda e: e.engine_nop(),
             waits=[(g("tX1"), g("tX1").n)] + [(t, t.n) for t in g("tWT")])
        return _emit(nc, B, ctx)
    stat_finalize(sum2, ssq2, nmu2, rst2, NTILE)
    B.op("vector", lambda e: e.tensor_mul(sfold[:], rst2[:], rst2[:]),
         waits=[g("cRSD")])
    B.op("vector", lambda e: e.tensor_mul(sfold[:], stat_t[:], sfold[:]),
         incs=[g("cSF1")])
    B.op("scalar", lambda e: e.activation(sfold[:], sfold[:], AF.Ln, bias=eps_t[:, 0:1]),
         waits=[g("cSF1")])
    B.op("scalar", lambda e: e.activation(sfold[:], sfold[:], AF.Exp, scale=-0.5),
         incs=[g("cSF2")])
    B.op("vector", lambda e: e.tensor_mul(sfold[:], rst2[:], sfold[:]),
         waits=[g("cSF2")], incs=[g("cRSD")])
    B.op("vector", lambda e: e.tensor_mul(nbf[:], nmu2[:], sfold[:]), incs=[g("cNB")])

    B.op("gpsimd", lambda e: e.dma_start(
        out=binrow[0:1, :], in_=b_in.rearrange("(a f) -> a f", a=1)),
        incs=[(g("sCB"), 16)], dma=True)
    for cn in range(8):
        B.op("gpsimd", lambda e, cn=cn: e.dma_start(
            out=win_sb[:, cn], in_=W_in[cn * 128:(cn + 1) * 128, :]),
            incs=[(g("sWI"), 16)], dma=True)
    WIN_LD = g("sWI").n

    cNRM, cTP, cTE = g("cNRM"), g("cTP"), g("cTE")
    cPE1, cVE, cKQ = g("cPE1"), g("cVE"), g("cKQ")
    nrm_done = {}
    xni_tau = {}
    bank_reader = {}

    def p1b_norm(sx):
        own = sx < NST_OWN
        for tau in range(4):
            t = sx * 4 + tau
            nd = nrm_done.get(t - 4) or []
            B.op("sync", lambda e, t=t: e.dma_start(
                out=x1r[:, t % 4], in_=x1_dram[t * 128:(t + 1) * 128, :]),
                waits=[(g("tX1"), g("tX1").n)] + list(nd),
                incs=[(g("sQ")[t % 4], 16)])
        for tau in range(4):
            t = sx * 4 + tau
            sl = tau % 2
            B.op("scalar", lambda e, t=t, sl=sl: e.activation(
                zn_tm[:, sl], x1r[:, t % 4], AF.Identity,
                scale=sfold[:, t:t + 1], bias=nbf[:, t:t + 1]),
                waits=[g("sQ")[t % 4], g("cNB"),
                       (cTP, max(0, cTP.n - 1))], incs=[cNRM])
            if own:
                B.op("vector", lambda e, t=t: e.tensor_scalar(
                    xm_tmp[:], x1r[:, t % 4], nmu2[:, t:t + 1], rst2[:, t:t + 1],
                    ALU.add, ALU.mult),
                    waits=[g("sQ")[t % 4], g("cRSD")])
                B.op("vector", lambda e, t=t: e.tensor_tensor(
                    xm_stg[:, t % 2], x1r[:, t % 4], xm_tmp[:], ALU.add),
                    waits=[g("tM")[t % 2]], incs=[cNRM])
                B.op("gpsimd", lambda e, t=t: e.dma_start(
                    out=xm_dram[t * 128:(t + 1) * 128, :], in_=xm_stg[:, t % 2]),
                    waits=[cNRM], incs=[(g("tM")[t % 2], 16)], dma=True)
            nrm_done[t] = cNRM.all()
            pt = pTv[sl]
            for fc in range(8):
                B.op("tensor", lambda e, sl=sl, fc=fc, pt=pt: e.transpose(
                    pt[:].rearrange("p (f j) -> p f j", f=8)[:, fc],
                    zn_tm[:, sl, fc * 128:(fc + 1) * 128], ident[:]),
                    waits=([cNRM, (cTE, 1)] if fc == 0 else []),
                    incs=([cTP] if fc == 7 else []))
            for fc in range(8):
                B.op("vector", lambda e, sx=sx, tau=tau, fc=fc, pt=pt: e.tensor_scalar(
                    xnifm[:, sx % 2, fc, tau * 128:(tau + 1) * 128],
                    pt[:].rearrange("p (f j) -> p f j", f=8)[:, fc],
                    lng_t[:, fc:fc + 1], lnb_t[:, fc:fc + 1], ALU.mult, ALU.add),
                    waits=([cTP, (cPE1, cPE1.n)] if fc == 0 else []),
                    incs=([cTE] if fc == 7 else []))
            xni_tau[(sx, tau)] = cTE.back(0)

    def p1b_mm(sx):
        own = sx < NST_OWN
        xb = sx % 2
        def brd(bank):
            r = bank_reader.get(id(bank))
            return [r] if r else []
        for tau in range(4):
            for eg in range(4):
                pv_ = pA if (tau * 4 + eg) % 2 == 0 else pB
                for fc in range(8):
                    B.op("tensor", lambda e, xb=xb, tau=tau, eg=eg, fc=fc, pv_=pv_: e.matmul(
                        pv_[:], xnifm[:, xb, fc, tau * 128:(tau + 1) * 128],
                        win_sb[:, fc, E + eg * 512:E + (eg + 1) * 512],
                        start=(fc == 0), stop=False),
                        waits=(([xni_tau[(sx, tau)], (g("sWI"), WIN_LD)]
                                + brd(pv_)) if fc == 0 else []))
                B.op("tensor", lambda e, eg=eg, pv_=pv_: e.matmul(
                    pv_[:], ones[0:1, 0:128],
                    binrow[0:1, E + eg * 512:E + (eg + 1) * 512],
                    start=False, stop=True),
                    waits=[(g("sCB"), g("sCB").n)], incs=[cPE1])
                B.op("scalar", lambda e, tau=tau, eg=eg, pv_=pv_: e.activation(
                    v_stg[:, tau % 2, eg * 512:(eg + 1) * 512], pv_[:], AF.Silu),
                    waits=[(cPE1, cPE1.n), g("tV")[tau % 2]],
                    incs=[cVE])
                bank_reader[id(pv_)] = cVE.back(0)
            B.op("gpsimd", lambda e, sx=sx, tau=tau: e.dma_start(
                out=v_dram[:, (sx * 4 + tau) * E:(sx * 4 + tau + 1) * E],
                in_=v_stg[:, tau % 2]), waits=[cVE], incs=[(g("tV")[tau % 2], 16)], dma=True)
        zsl = sx % 2
        for fc in range(8):
            B.op("tensor", lambda e, xb=xb, fc=fc: e.matmul(
                pA[:], win_sb[:, fc, 2 * E:2 * E + S], xnifm[:, xb, fc],
                start=(fc == 0), stop=(fc == 7)),
                waits=(brd(pA) if fc == 0 else []),
                incs=([cPE1] if fc == 7 else []))
        B.op("scalar", lambda e, zsl=zsl: e.activation(
            z_scr[:, zsl], pA[:], AF.Silu, bias=bz_t[:, 0:1]),
            waits=[(cPE1, cPE1.n), cKQ], incs=[cVE])
        bank_reader[id(pA)] = cVE.back(0)
        B.op("vector", lambda e, zsl=zsl: e.tensor_scalar(
            k_stg[:, zsl], z_scr[:, zsl], gk_t[:, 0:1], bk_t[:, 0:1],
            ALU.mult, ALU.add),
            waits=[cVE, g("tK")[zsl]], incs=[cKQ])
        B.op("gpsimd", lambda e, sx=sx, zsl=zsl: e.dma_start(
            out=k_dram[:, sx * 512:(sx + 1) * 512], in_=k_stg[:, zsl]),
            waits=[cKQ], incs=[(g("tK")[zsl], 16)], dma=True)
        if own:
            B.op("vector", lambda e, sx=sx, zsl=zsl: e.tensor_scalar(
                q_sb[:, sx * 512:(sx + 1) * 512], z_scr[:, zsl], gq_t[:, 0:1],
                bq_t[:, 0:1], ALU.mult, ALU.add), incs=[cKQ])
            for ecn in range(16):
                pu = pA if ecn % 2 == 0 else pB
                for fc in range(8):
                    B.op("tensor", lambda e, xb=xb, ecn=ecn, fc=fc, pu=pu: e.matmul(
                        pu[:], win_sb[:, fc, ecn * 128:(ecn + 1) * 128],
                        xnifm[:, xb, fc], start=(fc == 0), stop=(fc == 7)),
                        waits=(([cKQ] if ecn == 0 else []) + (brd(pu) if fc == 0 else [])),
                        incs=([cPE1] if fc == 7 else []))
                B.op("scalar", lambda e, ecn=ecn, pu=pu: e.activation(
                    u_stg[:, ecn % 4], pu[:], AF.Silu, bias=bu_t[:, ecn:ecn + 1]),
                    waits=[(cPE1, cPE1.n), g("tU")[ecn % 4]],
                    incs=[cVE])
                bank_reader[id(pu)] = cVE.back(0)
                B.op("gpsimd", lambda e, sx=sx, ecn=ecn: e.dma_start(
                    out=u_dram[ecn * 128:(ecn + 1) * 128, sx * 512:(sx + 1) * 512],
                    in_=u_stg[:, ecn % 4]), waits=[cVE], incs=[(g("tU")[ecn % 4], 16)], dma=True)

    p1b_norm(0)
    for sx in range(NST):
        if sx + 1 < NST:
            p1b_norm(sx + 1)
        p1b_mm(sx)

    if upto < 4:
        B.op("gpsimd", lambda e: e.engine_nop(),
             waits=[(t, t.n) for t in g("tV") + g("tK") + g("tU") + g("tM")])
        return _emit(nc, B, ctx)
    KB, NKB, CPB = 1024, 16, 8
    cSC, cRL, cP, cPV, cOE = g("cSC"), g("cRL"), g("cP"), g("cPV"), g("cOE")
    cOU, cY, cX2, cSTA = g("cOU"), g("cY"), g("cX2"), g("cSTA")

    for half in range(2):
        B.barrier()
        QOFF = half * 1024
        pdone = {}

        def emit_scores(kb, QOFF=None):
            base = (kb % 2) * CPB
            B.op("sync", lambda e, kb=kb: e.dma_start(
                out=kst[:, kb % 2], in_=k_dram[:, kb * KB:(kb + 1) * KB]),
                waits=[(t, t.n) for t in g("tK")] + [(cP, 16)],
                incs=[(g("sK")[kb % 2], 16)])
            B.op("sync", lambda e, kb=kb, base=base: e.dma_start(
                out=vring[:, base:base + CPB].rearrange("p c e -> p (c e)"),
                in_=v_dram[:, kb * CPB * E:(kb + 1) * CPB * E]),
                waits=[(t, t.n) for t in g("tV")] + [(cPV, cPV.n)],
                incs=[(g("sV")[kb % 2], 16)])
            psl = kb % 2
            for kt in range(CPB):
                for qs in range(2):
                    ps_ = pA if qs == 0 else pB
                    B.op("tensor", lambda e, kb=kb, kt=kt, qs=qs, ps_=ps_, QOFF=QOFF: e.matmul(
                        ps_[:], kst[:, kb % 2, kt * 128:(kt + 1) * 128],
                        q_sb[:, QOFF + qs * 512:QOFF + (qs + 1) * 512],
                        start=True, stop=True),
                        waits=([g("sK")[kb % 2], g("cKQ"), (cRL, 1)]
                               if (kt == 0 and qs == 0) else [(cRL, 1)]),
                        incs=[cSC])
                    rsl = (kt % 2) * 2 + qs
                    B.op("vector", lambda e, rsl=rsl, ps_=ps_: e.tensor_scalar_max(
                        rl_scr[:, rsl], ps_[:], 0.0),
                        waits=[(cSC, cSC.n), (cP, 3)], incs=[cRL])
                    B.op("scalar", lambda e, psl=psl, kt=kt, qs=qs, rsl=rsl: e.activation(
                        p_sb[:, psl, kt, qs * 512:(qs + 1) * 512], rl_scr[:, rsl],
                        AF.Square),
                        waits=[cRL, (cPV, cPV.n)], incs=[cP])
            pdone[kb] = cP.back(0)

        def emit_pv(kb):
            base = (kb % 2) * CPB
            psl = kb % 2
            pvslots = [pC[:, 0:512], pC[:, 512:1024], pD[:, 0:512], pD[:, 512:1024]]
            for ecn in range(16):
                for qs in range(2):
                    slot = pvslots[(ecn * 2 + qs) % 4]
                    for kt in range(CPB):
                        B.op("tensor", lambda e, base=base, ecn=ecn, qs=qs, kt=kt, slot=slot, psl=psl: e.matmul(
                            slot, vring[:, base + kt, ecn * 128:(ecn + 1) * 128],
                            p_sb[:, psl, kt, qs * 512:(qs + 1) * 512],
                            start=(kt == 0), stop=(kt == CPB - 1)),
                            waits=([pdone[kb], g("sV")[kb % 2],
                                    (cOE, 3)] if kt == 0 else []),
                            incs=([cPV] if kt == CPB - 1 else []))
                    if kb == 0:
                        B.op("vector", lambda e, ecn=ecn, qs=qs, slot=slot: e.tensor_copy(
                            o_sb[:, ecn, qs * 512:(qs + 1) * 512], slot),
                            waits=[(cPV, cPV.n)], incs=[cOE])
                    else:
                        B.op("vector", lambda e, ecn=ecn, qs=qs, slot=slot: e.tensor_tensor(
                            o_sb[:, ecn, qs * 512:(qs + 1) * 512],
                            o_sb[:, ecn, qs * 512:(qs + 1) * 512], slot, ALU.add),
                            waits=[(cPV, cPV.n)], incs=[cOE])

        emit_scores(0, QOFF=QOFF)
        for kb in range(NKB):
            if kb + 1 < NKB:
                emit_scores(kb + 1, QOFF=QOFF)
            emit_pv(kb)
        OE_ALL = cOE.back(0)
        for cn in range(16):
            B.op("gpsimd", lambda e, cn=cn: e.dma_start(
                out=wout_sb[:, cn], in_=W_out[cn * 128:(cn + 1) * 128, :]),
                waits=([(cPV, cPV.n - 32)] if cn == 0 else []),
                incs=[(g("sWO"), 16)], dma=True)
        WOUT_LD = g("sWO").n
        for ecn in range(16):
            B.op("sync", lambda e, ecn=ecn, QOFF=QOFF: e.dma_start(
                out=u_ld[:, ecn % 2],
                in_=u_dram[ecn * 128:(ecn + 1) * 128, QOFF:QOFF + 1024]),
                waits=[(t, t.n) for t in g("tU")] + [(cOU, 1), (cPV, cPV.n - 32)],
                incs=[(g("sU")[ecn % 2], 16)])
            B.op("vector", lambda e, ecn=ecn: e.tensor_tensor(
                ou_bf[:, ecn], u_ld[:, ecn % 2], o_sb[:, ecn], ALU.mult),
                waits=[g("sU")[ecn % 2], (cOE, 30 - 2 * ecn), (cPV, cPV.n)],
                incs=[cOU])
        OU_ALL = cOU.back(0)
        for qp in range(4):
            for tau, pd in ((0, pC), (1, pD)):
                qt = qp * 2 + tau
                tq = half * 8 + qt
                for ecn in range(16):
                    for fh in range(2):
                        B.op("tensor", lambda e, ecn=ecn, qt=qt, fh=fh, pd=pd: e.matmul(
                            pd[:, fh * 512:(fh + 1) * 512],
                            ou_bf[:, ecn, qt * 128:(qt + 1) * 128],
                            wout_sb[:, ecn, fh * 512:(fh + 1) * 512],
                            start=(ecn == 0), stop=False),
                            waits=(([(cOU, 15 - ecn)] if fh == 0 else [])
                                   + ([(g("sWO"), WOUT_LD), (cX2, 1)]
                                      if (ecn == 0 and fh == 0) else [])))
                for fh in range(2):
                    B.op("tensor", lambda e, fh=fh, pd=pd: e.matmul(
                        pd[:, fh * 512:(fh + 1) * 512], ones[0:1, 0:128],
                        bout_bf[0:1, fh * 512:(fh + 1) * 512], start=False, stop=True),
                        incs=([cY] if fh == 1 else []))
                B.op("sync", lambda e, tq=tq, qt=qt: e.dma_start(
                    out=xm_ld[:, qt % 2], in_=xm_dram[tq * 128:(tq + 1) * 128, :]),
                    waits=[(t, t.n) for t in g("tM")] + [(cX2, 1)],
                    incs=[(g("sM")[qt % 2], 16)])
                B.op("vector", lambda e, qt=qt, pd=pd: e.tensor_tensor(
                    x2_stg[:, qt % 2], xm_ld[:, qt % 2], pd[:], ALU.add),
                    waits=[g("sM")[qt % 2], (cY, cY.n),
                           g("tX2")[qt % 2]], incs=[cX2])
                B.op("vector", lambda e, tq=tq, qt=qt: e.reduce_sum(
                    sum3[:, tq:tq + 1], x2_stg[:, qt % 2], axis=AX), incs=[cSTA])
                B.op("scalar", lambda e, tq=tq, qt=qt: e.activation(
                    sq3_scr[:], x2_stg[:, qt % 2], AF.Square,
                    accum_out=ssq3[:, tq:tq + 1]), waits=[cX2], incs=[cSTA])
                B.op("gpsimd", lambda e, tq=tq, qt=qt: e.dma_start(
                    out=x2_dram[tq * 128:(tq + 1) * 128, :], in_=x2_stg[:, qt % 2]),
                    waits=[cX2, cSTA], incs=[(g("tX2")[qt % 2], 16)], dma=True)

    B.barrier()
    if upto < 5:
        B.op("gpsimd", lambda e: e.engine_nop(),
             waits=[(t, t.n) for t in g("tX2")])
        return _emit(nc, B, ctx)
    B.op("sync", lambda e: e.dma_start(
        out=wgT[:].rearrange("p c h -> p (c h)"), in_=wgT_dram),
        incs=[(g("sR"), 16)])
    B.op("sync", lambda e: e.dma_start(
        out=wuT[:].rearrange("p c h -> p (c h)"), in_=wuT_dram),
        incs=[(g("sR"), 16)])
    W_RELOAD = g("sR").n
    stat_finalize(sum3, ssq3, nmu3, rst3, 16, nbt=nb3)
    ffn_pass(x2_dram, out, NST_OWN, nmu3, rst3, nb3, False, g("tOUT"),
             first_gate_waits=[(g("sR"), W_RELOAD)])

    B.op("gpsimd", lambda e: e.engine_nop(), waits=[(g("tOUT"), g("tOUT").n)])
    return _emit(nc, B, ctx)


def _emit(nc, B, ctx):
    with nc.Block() as blk:
        @blk.sync
        def _(e):
            for f in B.streams["sync"]:
                f(e)

        @blk.gpsimd
        def _(e):
            for f in B.streams["gpsimd"]:
                f(e)

        @blk.tensor
        def _(e):
            for f in B.streams["tensor"]:
                f(e)

        @blk.vector
        def _(e):
            for f in B.streams["vector"]:
                f(e)

        @blk.scalar
        def _(e):
            for f in B.streams["scalar"]:
                f(e)

    ctx.close()
    return nc


def kernel(**inputs):
    x = np.asarray(inputs["L0_attn_x"]).reshape(T_ALL, D)
    nc = build_nc(dbg=False)
    in_maps = []
    for c in range(NC_N):
        m = {k: np.ascontiguousarray(np.asarray(v)) for k, v in inputs.items()
             if k != "L0_attn_x"}
        m["L0_attn_x"] = np.ascontiguousarray(np.roll(x, -T_OWN * c, axis=0))
        in_maps.append(m)
    res = run_bass_kernel_spmd(nc, in_maps, core_ids=list(range(NC_N)))
    o = np.empty((T_ALL, D), dtype=np.float32)
    for c in range(NC_N):
        o[c * T_OWN:(c + 1) * T_OWN] = res.results[c]["out"]
    return o.reshape(1, T_ALL, D)


# version 08
# speedup vs baseline: 1.1589x; 1.1093x over previous
import numpy as np
import concourse.bass as bass
import concourse.mybir as mybir
from concourse.bass_utils import run_bass_kernel_spmd
from contextlib import ExitStack

F32 = mybir.dt.float32
BF16 = mybir.dt.bfloat16
I32 = mybir.dt.int32
AF = mybir.ActivationFunctionType
ALU = mybir.AluOpType
AX = mybir.AxisListType.X

T_ALL, T_OWN, D, H, E, S, FUVZ = 16384, 2048, 1024, 3072, 2048, 128, 4224
LSEQ = 16384.0
EPS = 1e-5
NC_N = 8
ST = 512
NST = T_ALL // ST
NST_OWN = T_OWN // ST
NTILE = T_ALL // 128
DQ = 1.0 / 127.0
SB_BASE = 16512
SB_LIMIT = 229376


class Sem:
    __slots__ = ("h", "n", "name")


class Mile:
    __slots__ = ("name", "hist", "targets")

    def __init__(self, name):
        self.name = name
        self.hist = []
        self.targets = {}

    def rec(self, h, n):
        self.hist.append((h, n))
        self.targets[id(h)] = (h, n)

    def back(self, k):
        return self.hist[-1 - k] if len(self.hist) > k else None

    def all(self):
        return list(self.targets.values())


class Builder:
    CHAINED = ("vector", "scalar", "gpsimd")

    def __init__(self, nc, ctx):
        self.nc, self.ctx = nc, ctx
        self.streams = {k: [] for k in ("sync", "gpsimd", "tensor", "vector", "scalar")}
        self.pending = {k: [] for k in self.streams}
        self.all_sems = []
        self.chains = {}

    def sem(self, name):
        s = Sem()
        s.h = self.ctx.enter_context(self.nc.semaphore(name))
        s.n = 0
        s.name = name
        self.all_sems.append(s)
        return s

    def snap(self):
        return [(s, s.n) for s in self.all_sems if s.n > 0]

    def barrier(self, snap=None):
        snap = self.snap() if snap is None else snap
        for k in self.pending:
            self.pending[k] = list(snap)

    def op(self, eng, builder, waits=(), incs=(), dma=False):
        wl = {}

        def add(h, v):
            if v > 0:
                wl[id(h)] = (h, max(wl[id(h)][1], v) if id(h) in wl else v)

        for w in list(self.pending[eng]) + list(waits):
            if isinstance(w, Sem):
                add(w.h, w.n)
            elif isinstance(w, Mile):
                for h, v in w.all():
                    add(h, v)
            elif isinstance(w, tuple) and isinstance(w[0], Sem):
                add(w[0].h, w[1])
            elif isinstance(w, tuple) and isinstance(w[0], Mile):
                b = w[0].back(w[1])
                if b is not None:
                    add(b[0], b[1])
            elif isinstance(w, tuple) and len(w) == 2:
                add(w[0], w[1])
            else:
                raise TypeError(f"bad wait {w}")
        self.pending[eng] = []

        chained = eng in self.chains and not dma
        il = []
        if chained:
            ch = self.chains[eng]
            if ch.n > 0:
                add(ch.h, ch.n)
            ch.n += 1
            il.append((ch.h, 1))
            for i in incs:
                m = i if not isinstance(i, tuple) else i[0]
                assert isinstance(m, Mile), f"chained op can only inc Mile, got {m}"
                m.rec(ch.h, ch.n)
        else:
            for i in incs:
                s2, k = i if isinstance(i, tuple) else (i, 1)
                assert isinstance(s2, Sem), f"unchained op needs Sem inc, got {s2}"
                s2.n += k
                il.append((s2.h, k))
            assert len(il) <= 1, "one sem update per instruction"
        wl = list(wl.values())

        def f(e, b=builder, wl=wl, il=il):
            for h, v in wl:
                e.wait_ge(h, v)
            ins = b(e)
            for h, k in il:
                ins.then_inc(h, k)
        self.streams[eng].append(f)


def build_nc(dbg=False, upto=5):
    nc = bass.Bass("TRN2", target_bir_lowering=False, debug=False)
    ctx = ExitStack()

    x_in = nc.dram_tensor("L0_attn_x", [T_ALL, D], F32, kind="ExternalInput").ap()
    ln_g = nc.dram_tensor("L0_attn_ln_g", [D], F32, kind="ExternalInput").ap()
    ln_b = nc.dram_tensor("L0_attn_ln_b", [D], F32, kind="ExternalInput").ap()
    W_in = nc.dram_tensor("L0_attn_W_in", [D, FUVZ], F32, kind="ExternalInput").ap()
    b_in = nc.dram_tensor("L0_attn_b_in", [FUVZ], F32, kind="ExternalInput").ap()
    W_out = nc.dram_tensor("L0_attn_W_out", [E, D], F32, kind="ExternalInput").ap()
    b_out = nc.dram_tensor("L0_attn_b_out", [D], F32, kind="ExternalInput").ap()
    gam_q = nc.dram_tensor("L0_attn_gamma_q", [S], F32, kind="ExternalInput").ap()
    bet_q = nc.dram_tensor("L0_attn_beta_q", [S], F32, kind="ExternalInput").ap()
    gam_k = nc.dram_tensor("L0_attn_gamma_k", [S], F32, kind="ExternalInput").ap()
    bet_k = nc.dram_tensor("L0_attn_beta_k", [S], F32, kind="ExternalInput").ap()
    wg_i = nc.dram_tensor("L0_ffn_w_gate", [H, D], I32, kind="ExternalInput").ap()
    wu_i = nc.dram_tensor("L0_ffn_w_up", [H, D], I32, kind="ExternalInput").ap()
    wd_i = nc.dram_tensor("L0_ffn_w_down", [D, H], I32, kind="ExternalInput").ap()
    out = nc.dram_tensor("out", [T_OWN, D], F32, kind="ExternalOutput").ap()

    def dscr(name, shape, dtype, as_out=False):
        kind = "ExternalOutput" if (dbg and as_out) else "Internal"
        return nc.dram_tensor(name, shape, dtype, kind=kind).ap()

    x1_dram = dscr("dbg_x1", [T_ALL, D], F32, as_out=True)
    v_dram = dscr("dbg_v", [128, NTILE * E], BF16, as_out=True)
    k_dram = dscr("dbg_k", [S, T_ALL], BF16, as_out=True)
    u_dram = dscr("dbg_u", [E, T_OWN], BF16, as_out=True)
    xm_dram = dscr("dbg_xm", [T_OWN, D], F32, as_out=True)
    x2_dram = dscr("dbg_x2", [T_OWN, D], F32, as_out=True)
    wgT_dram = dscr("wgT_dram", [128, 8 * H], BF16)
    wuT_dram = dscr("wuT_dram", [128, 8 * H], BF16)
    wdT_dram = dscr("wdT_dram", [128, 24 * D], BF16)

    class Cur:
        def __init__(self, base):
            self.off = base

        def al(self, name, shape, dtype):
            nbytes = int(np.prod(shape[1:])) * mybir.dt.size(dtype)
            o = self.off
            self.off = (self.off + nbytes + 31) & ~31
            assert self.off <= SB_LIMIT, f"SBUF overflow at {name}: {self.off}"
            return nc.alloc_sbuf_tensor_at(name, list(shape), dtype, offset=o).ap()

    cc = Cur(SB_BASE)
    ident = cc.al("ident", [128, 128], BF16)
    ones = cc.al("ones", [1, 512], BF16)
    bout_bf = cc.al("bout_bf", [1, D], BF16)
    bz_t = cc.al("bz", [128, 1], F32)
    bu_t = cc.al("bu", [128, 16], F32)
    gq_t = cc.al("gq", [128, 1], F32)
    bq_t = cc.al("bq", [128, 1], F32)
    gk_t = cc.al("gk", [128, 1], F32)
    bk_t = cc.al("bk", [128, 1], F32)
    lng_t = cc.al("lng", [128, 8], F32)
    lnb_t = cc.al("lnb", [128, 8], F32)
    eps_t = cc.al("eps", [128, 1], F32)
    sum1 = cc.al("sum1", [128, NTILE], F32)
    ssq1 = cc.al("ssq1", [128, NTILE], F32)
    nmu1 = cc.al("nmu1", [128, NTILE], F32)
    rst1 = cc.al("rst1", [128, NTILE], F32)
    sum2 = cc.al("sum2", [128, NTILE], F32)
    ssq2 = cc.al("ssq2", [128, NTILE], F32)
    nmu2 = cc.al("nmu2", [128, NTILE], F32)
    rst2 = cc.al("rst2", [128, NTILE], F32)
    sfold = cc.al("sfold", [128, NTILE], F32)
    stat_t = cc.al("stat_t", [128, NTILE], F32)
    sum3 = cc.al("sum3", [128, 16], F32)
    ssq3 = cc.al("ssq3", [128, 16], F32)
    nmu3 = cc.al("nmu3", [128, 16], F32)
    rst3 = cc.al("rst3", [128, 16], F32)
    nb1 = cc.al("nb1", [128, NTILE], F32)
    rst1m = cc.al("rst1m", [128, NTILE], F32)
    nb1m = cc.al("nb1m", [128, NTILE], F32)
    rst3m = cc.al("rst3m", [128, 16], F32)
    nb3m = cc.al("nb3m", [128, 16], F32)
    nbf = cc.al("nbf", [128, NTILE], F32)
    nb3 = cc.al("nb3", [128, 16], F32)
    q_sb = cc.al("q_sb", [128, T_OWN], BF16)
    PB = (cc.off + 63) & ~63

    ca = Cur(PB)
    wgT = ca.al("wgT", [128, 8, H], BF16)
    wuT = ca.al("wuT", [128, 8, H], BF16)
    SCR = ca.off
    c = Cur(SCR)
    wd_bf = c.al("wd_bf", [128, 8, H // 4], BF16)
    wint = c.al("wint", [128, 2, 3, D], I32)
    wbf = c.al("wbf", [128, 2, 3, D], BF16)
    rowst = c.al("rowst", [128, 2, 8, 128], BF16)
    xs_t = c.al("xs_t", [128, 2, 4, D], F32)
    sq0 = c.al("sq0", [128, D], F32)
    c = Cur(SCR)
    wdTr = c.al("wdTr", [128, 12, D], BF16)
    h_sb = c.al("h_sb", [128, 12, ST], BF16)
    g_bf = c.al("g_bf", [128, 2, ST], BF16)
    xn_t = c.al("xn_t", [128, 4, D], F32)
    x1a = c.al("x1a", [128, 4, D], F32)
    xn1tm = c.al("xn1tm", [128, 2, D], BF16)
    xn1fm = c.al("xn1fm", [128, 2, 8, ST], BF16)
    sq1a = c.al("sq1a", [128, D], F32)
    c = Cur(PB)
    win_sb = c.al("win_sb", [128, 8, FUVZ], BF16)
    binrow = c.al("binrow", [1, FUVZ], BF16)
    x1r = c.al("x1r", [128, 4, D], F32)
    zn_tm = c.al("zn_tm", [128, 2, D], BF16)
    xnifm = c.al("xnifm", [128, 2, 8, ST], BF16)
    v_stg = c.al("v_stg", [128, 2, E], BF16)
    z_scr = c.al("z_scr", [128, 2, ST], F32)
    k_stg = c.al("k_stg", [128, 2, ST], BF16)
    u_stg = c.al("u_stg", [128, 4, ST], BF16)
    xm_stg = c.al("xm_stg", [128, 2, D], F32)
    xm_tmp = c.al("xm_tmp", [128, D], F32)
    c = Cur(PB)
    o_sb = c.al("o_sb", [128, 16, 1024], F32)
    ALIAS = c.off
    vring = c.al("vring", [128, 16, E], BF16)
    p_sb = c.al("p_sb", [128, 2, 8, 1024], BF16)
    rl_scr = c.al("rl_scr", [128, 4, 512], F32)
    kst = c.al("kst", [128, 2, 1024], BF16)
    c = Cur(ALIAS)
    wout_sb = c.al("wout_sb", [128, 16, 1024], BF16)
    ou_bf = c.al("ou_bf", [128, 16, 1024], BF16)
    u_ld = c.al("u_ld", [128, 2, 1024], BF16)
    xm_ld = c.al("xm_ld", [128, 2, D], F32)
    x2_stg = c.al("x2_stg", [128, 2, D], F32)
    sq3_scr = c.al("sq3", [128, D], F32)

    pA = ctx.enter_context(nc.psum_tensor("pA", [128, 512], F32))
    pB = ctx.enter_context(nc.psum_tensor("pB", [128, 512], F32))
    pC = ctx.enter_context(nc.psum_tensor("pC", [128, 1024], F32))
    pD = ctx.enter_context(nc.psum_tensor("pD", [128, 1024], F32))
    pT = ctx.enter_context(nc.psum_tensor("pT", [128, 1024], BF16))
    pT2 = ctx.enter_context(nc.psum_tensor("pT2", [128, 1024], BF16))
    pTv = [pT, pT2]

    B = Builder(nc, ctx)
    B.chains = {k: B.sem("chain_" + k) for k in ("vector", "scalar", "gpsimd")}
    sm = {n: B.sem(n) for n in (
        "sC", "sCB", "sWI", "sWO", "sR",
        "tX1", "tOUT",
        "cTP", "cGA", "cUP", "cDWN", "cPE1", "cSC", "cPV", "cY")}
    for base, k in (("sW", 2), ("sXs", 2), ("sXr", 4), ("sQ", 4), ("sWD", 3),
                    ("sV", 2), ("sK", 2), ("sU", 2), ("sM", 2),
                    ("tV", 2), ("tK", 2), ("tU", 4), ("tM", 2), ("tX2", 2),
                    ("tWT", 3)):
        sm[base] = [B.sem(f"{base}{i}") for i in range(k)]
    for n in ("cDEQ", "cTE", "cNRM", "cSIL", "cH", "cADD", "cSTA", "cRSD",
              "cVE", "cKQ", "cRL", "cP", "cOE", "cOU", "cX2", "cMISC",
              "cSF1", "cSF2", "cDCm", "cNB"):
        sm[n] = Mile(n)
    g = lambda n: sm[n]

    def ld_const(dst, src):
        B.op("gpsimd", lambda e, d=dst, s=src: e.dma_start(out=d, in_=s),
             incs=[(g("sC"), 16)], dma=True)

    ld_const(bz_t[:, 0], b_in[2 * E:2 * E + S])
    for ecn in range(16):
        ld_const(bu_t[:, ecn], b_in[ecn * 128:(ecn + 1) * 128])
    ld_const(gq_t[:, 0], gam_q)
    ld_const(bq_t[:, 0], bet_q)
    ld_const(gk_t[:, 0], gam_k)
    ld_const(bk_t[:, 0], bet_k)
    for cn in range(8):
        ld_const(lng_t[:, cn], ln_g[cn * 128:(cn + 1) * 128])
        ld_const(lnb_t[:, cn], ln_b[cn * 128:(cn + 1) * 128])
    B.op("gpsimd", lambda e: e.dma_start(
        out=bout_bf[0:1, :], in_=b_out.rearrange("(a f) -> a f", a=1)),
        incs=[(g("sCB"), 16)], dma=True)
    NCONST = g("sC").n

    B.op("gpsimd", lambda e: e.memset(ident[:], 0.0))
    B.op("gpsimd", lambda e: e.affine_select(
        out=ident[:], in_=ident[:], compare_op=ALU.not_equal, fill=1.0,
        base=0, pattern=[[-1, 128]], channel_multiplier=1), incs=[g("cMISC")])
    B.op("gpsimd", lambda e: e.memset(ones[:], 1.0), incs=[g("cMISC")])
    B.op("vector", lambda e: e.memset(eps_t[:], EPS), incs=[g("cMISC")])
    B.op("scalar", lambda e: e.mul(gq_t[:], gq_t[:], 1.0 / LSEQ),
         waits=[(g("sC"), NCONST)], incs=[g("cMISC")])
    B.op("scalar", lambda e: e.mul(bq_t[:], bq_t[:], 1.0 / LSEQ), incs=[g("cMISC")])
    IDENT_OK = g("cMISC").all()

    def prep_gu(w_src, wT_dst):
        for grp in range(8):
            sl = grp % 2
            B.op("gpsimd", lambda e, grp=grp, sl=sl, w=w_src: e.dma_start(
                out=wbf[:, sl],
                in_=w[grp * 384:(grp + 1) * 384, :].rearrange("(c p) f -> p c f", p=128)),
                waits=[(g("cTP"), max(0, g("cTP").n - 3))], incs=[(g("sW")[sl], 16)],
                dma=True)
            for j in range(3):
                r = grp * 3 + j
                pt = pTv[r % 2]
                for fc in range(8):
                    B.op("tensor", lambda e, sl=sl, j=j, fc=fc, pt=pt: e.transpose(
                        pt[:].rearrange("p (f j2) -> p f j2", f=8)[:, fc],
                        wbf[:, sl, j, fc * 128:(fc + 1) * 128], ident[:]),
                        waits=(IDENT_OK + [g("sW")[sl], (g("cTE"), 1)]
                               if fc == 0 else []),
                        incs=([g("cTP")] if fc == 7 else []))
                B.op("vector", lambda e, r=r, pt=pt, wT=wT_dst: e.tensor_copy(
                    wT[:, :, r * 128:(r + 1) * 128],
                    pt[:].rearrange("p (f j2) -> p f j2", f=8)),
                    waits=[g("cTP")], incs=[g("cTE")])

    prep_gu(wg_i, wgT)
    prep_gu(wu_i, wuT)
    B.op("gpsimd", lambda e: e.dma_start(
        out=wgT_dram, in_=wgT[:].rearrange("p c h -> p (c h)")),
        waits=[g("cTE")], incs=[(g("tWT")[2], 16)], dma=True)
    B.op("gpsimd", lambda e: e.dma_start(
        out=wuT_dram, in_=wuT[:].rearrange("p c h -> p (c h)")),
        incs=[(g("tWT")[2], 16)], dma=True)

    for ch in range(4):
        tp_guard = (g("cTP"), g("cTP").n)
        for r in range(8):
            sl = r % 2
            B.op("gpsimd", lambda e, r=r, sl=sl, ch=ch: e.dma_start(
                out=wd_bf[:, r, :],
                in_=wd_i[r * 128:(r + 1) * 128, ch * 768:(ch + 1) * 768]),
                waits=[tp_guard], incs=[(g("sW")[sl], 16)], dma=True)
        WD_DEQ = [(t, t.n) for t in g("sW")]
        for ci in range(6):
            cn = ch * 6 + ci
            sl = cn % 2
            pt = pTv[sl]
            for r in range(8):
                B.op("tensor", lambda e, ci=ci, r=r, pt=pt: e.transpose(
                    pt[:].rearrange("p (f j) -> p f j", f=8)[:, r],
                    wd_bf[:, r, ci * 128:(ci + 1) * 128], ident[:]),
                    waits=(WD_DEQ + [(g("cTE"), 1)] if r == 0 else []),
                    incs=([g("cTP")] if r == 7 else []))
            B.op("vector", lambda e, sl=sl, pt=pt: e.tensor_copy(
                rowst[:, sl], pt[:].rearrange("p (f j) -> p f j", f=8)),
                waits=[g("cTP"), g("tWT")[sl]], incs=[g("cTE")])
            B.op("gpsimd", lambda e, cn=cn, sl=sl: e.dma_start(
                out=wdT_dram[:, cn * D:(cn + 1) * D],
                in_=rowst[:, sl].rearrange("p r j -> p (r j)")),
                waits=[g("cTE")], incs=[(g("tWT")[sl], 16)], dma=True)
    TWT_ALL = [(t, t.n) for t in g("tWT")]

    STA_MID = None
    for grp in range(NTILE // 4):
        sl = grp % 2
        B.op("sync", lambda e, grp=grp, sl=sl: e.dma_start(
            out=xs_t[:, sl],
            in_=x_in[grp * 512:(grp + 1) * 512, :].rearrange("(c p) f -> p c f", p=128)),
            waits=[(g("cSTA"), 8), (g("cSTA"), 9)], incs=[(g("sXs")[sl], 16)])
        for j in range(4):
            t = grp * 4 + j
            B.op("vector", lambda e, sl=sl, t=t, j=j: e.reduce_sum(
                sum1[:, t:t + 1], xs_t[:, sl, j], axis=AX), waits=[g("sXs")[sl]],
                incs=[g("cSTA")])
            B.op("scalar", lambda e, sl=sl, t=t, j=j: e.activation(
                sq0[:], xs_t[:, sl, j], AF.Square, accum_out=ssq1[:, t:t + 1]),
                waits=[g("sXs")[sl]], incs=[g("cSTA")])
        if grp == NTILE // 8 - 1:
            STA_MID = g("cSTA").all()

    def stat_finalize(sumt, ssqt, nmut, rstt, n, nbt=None, lo=0, stat_waits=None, m127=None):
        sw = [g("cSTA")] if stat_waits is None else stat_waits
        sl_ = slice(lo, n)
        B.op("vector", lambda e, sl_=sl_: e.tensor_scalar_mul(
            nmut[:, sl_], sumt[:, sl_], -1.0 / D), waits=sw + IDENT_OK)
        B.op("vector", lambda e, sl_=sl_: e.tensor_mul(
            stat_t[:, sl_], nmut[:, sl_], nmut[:, sl_]))
        B.op("vector", lambda e, sl_=sl_: e.scalar_tensor_tensor(
            stat_t[:, sl_], ssqt[:, sl_], 1.0 / D, stat_t[:, sl_],
            ALU.mult, ALU.subtract),
            incs=[g("cSF1")])
        B.op("scalar", lambda e, sl_=sl_: e.activation(
            rstt[:, sl_], stat_t[:, sl_], AF.Ln, bias=eps_t[:, 0:1]),
             waits=sw + [g("cSF1")] + IDENT_OK)
        B.op("scalar", lambda e, sl_=sl_: e.activation(
            rstt[:, sl_], rstt[:, sl_], AF.Exp, scale=-0.5), incs=[g("cRSD")])
        if nbt is not None:
            B.op("vector", lambda e, sl_=sl_: e.tensor_mul(
                nbt[:, sl_], nmut[:, sl_], rstt[:, sl_]),
                 waits=[g("cRSD")], incs=[g("cNB")])
        if m127 is not None:
            rstm, nbm = m127
            B.op("vector", lambda e, sl_=sl_: e.tensor_scalar_mul(
                rstm[:, sl_], rstt[:, sl_], DQ), waits=[g("cRSD")])
            B.op("vector", lambda e, sl_=sl_: e.tensor_mul(
                nbm[:, sl_], nmut[:, sl_], rstm[:, sl_]), incs=[g("cNB")])

    stat_finalize(sum1, ssq1, nmu1, rst1, NTILE // 2, nbt=nb1, stat_waits=STA_MID, m127=(rst1m, nb1m))
    stat_finalize(sum1, ssq1, nmu1, rst1, NTILE, nbt=nb1, lo=NTILE // 2, m127=(rst1m, nb1m))

    c_glob = [0]
    add_done = {}

    def ffn_pass(x_src, y_dst, nst, nmu, rst, nbt, ln2_sums, tY, first_gate_waits=(),
                 nb_wait=None):
        sXr, sWD = g("sXr"), g("sWD")
        cNRM, cTP, cTE, cGA, cUP = g("cNRM"), g("cTP"), g("cTE"), g("cGA"), g("cUP")
        cSIL, cH, cADD, cSTA = g("cSIL"), g("cH"), g("cADD"), g("cSTA")
        cDCm, cDWN = g("cDCm"), g("cDWN")
        pd_rd = {}

        def norm_block(s2, first=False):
            for tau in range(4):
                t = s2 * 4 + tau
                ad = add_done.get((tY.name, t - 4))
                B.op("sync", lambda e, t=t, x=x_src: e.dma_start(
                    out=xn_t[:, t % 4], in_=x[t * 128:(t + 1) * 128, :]),
                    waits=([ad] if ad else []),
                    incs=[(sXr[t % 4], 16)])
            for tau in range(4):
                t = s2 * 4 + tau
                sl = tau % 2
                tt = t if nmu.shape[1] > 16 else t % 16
                B.op("scalar", lambda e, t=t, sl=sl, tt=tt, rst=rst, nbt=nbt: e.activation(
                    xn1tm[:, sl], xn_t[:, t % 4], AF.Identity,
                    scale=rst[:, tt:tt + 1], bias=nbt[:, tt:tt + 1]),
                    waits=[sXr[t % 4]]
                          + (nb_wait(t) if nb_wait else [g("cNB")])
                          + [(cTP, max(0, cTP.n - 1))],
                    incs=[cNRM])
                pt = pTv[sl]
                for fc in range(8):
                    B.op("tensor", lambda e, sl=sl, fc=fc, pt=pt: e.transpose(
                        pt[:].rearrange("p (f j) -> p f j", f=8)[:, fc],
                        xn1tm[:, sl, fc * 128:(fc + 1) * 128], ident[:]),
                        waits=(([cNRM, (cTE, 1)]
                                + (list(first_gate_waits) if (first and tau == 0) else []))
                               if fc == 0 else []),
                        incs=([cTP] if fc == 7 else []))
                B.op("vector", lambda e, s2=s2, tau=tau, pt=pt: e.tensor_copy(
                    xn1fm[:, s2 % 2].rearrange("p c (t j) -> p c t j", t=4)[:, :, tau],
                    pt[:].rearrange("p (f j) -> p f j", f=8)),
                    waits=[cTP, (cUP, cUP.n)], incs=[cTE])

        def half_block(s2, half):
            xb = s2 % 2
            for grp in range(3):
                cg = c_glob[0] + grp * 4
                dc = cg % 24
                slot = cg % 12
                B.op("sync", lambda e, dc=dc, slot=slot: e.dma_start(
                    out=wdTr[:, slot:slot + 4].rearrange("p c f -> p (c f)"),
                    in_=wdT_dram[:, dc * D:(dc + 4) * D]),
                    waits=TWT_ALL + [cDCm],
                    incs=[(sWD[(cg // 4) % 3], 16)])
            for ci in range(12):
                cn = half * 12 + ci
                for fc in range(8):
                    B.op("tensor", lambda e, xb=xb, cn=cn, fc=fc: e.matmul(
                        pA[:], wgT[:, fc, cn * 128:(cn + 1) * 128], xn1fm[:, xb, fc],
                        start=(fc == 0), stop=(fc == 7)),
                        waits=([XN_T[s2], cSIL] if fc == 0 else []),
                        incs=([cGA] if fc == 7 else []))
                for fc in range(8):
                    B.op("tensor", lambda e, xb=xb, cn=cn, fc=fc: e.matmul(
                        pB[:], wuT[:, fc, cn * 128:(cn + 1) * 128], xn1fm[:, xb, fc],
                        start=(fc == 0), stop=(fc == 7)),
                        waits=([cH] if fc == 0 else []),
                        incs=([cUP] if fc == 7 else []))
                B.op("scalar", lambda e, ci=ci: e.activation(
                    g_bf[:, ci % 2], pA[:], AF.Silu),
                    waits=[(cGA, cGA.n), (cH, 1)], incs=[cSIL])
                B.op("vector", lambda e, ci=ci: e.tensor_tensor(
                    h_sb[:, ci], g_bf[:, ci % 2], pB[:], ALU.mult),
                    waits=[(cUP, cUP.n), cSIL, cDCm],
                    incs=[cH])
            for pair in range(2):
                for tau, pd in ((pair * 2, pC), (pair * 2 + 1, pD)):
                    t = s2 * 4 + tau
                    for ci in range(12):
                        slot = (c_glob[0] + ci) % 12
                        for fh in range(2):
                            B.op("tensor", lambda e, ci=ci, tau=tau, fh=fh, pd=pd, slot=slot: e.matmul(
                                pd[:, fh * 512:(fh + 1) * 512],
                                h_sb[:, ci, tau * 128:(tau + 1) * 128],
                                wdTr[:, slot, fh * 512:(fh + 1) * 512],
                                start=(ci == 0), stop=(ci == 11)),
                                waits=(([cH] + [(w, w.n) for w in sWD]
                                        + (pd_rd.get(id(pd)) or []))
                                       if (ci == 0 and fh == 0) else []),
                                incs=([cDWN] if (ci == 11 and fh == 1) else []))
                    if half == 0:
                        B.op("vector", lambda e, t=t, tau=tau, pd=pd: e.scalar_tensor_tensor(
                            x1a[:, tau], pd[:], 0.5 * DQ, xn_t[:, t % 4],
                            ALU.mult, ALU.add),
                            waits=[(tY, tY.n), (cDWN, cDWN.n)],
                            incs=[cADD] + ([cDCm] if tau == 3 else []))
                        add_done[(tY.name, t)] = cADD.back(0)
                        pd_rd[id(pd)] = [cADD.back(0)]
                    else:
                        B.op("vector", lambda e, tau=tau, pd=pd: e.scalar_tensor_tensor(
                            x1a[:, tau], pd[:], 0.5 * DQ, x1a[:, tau],
                            ALU.mult, ALU.add),
                            waits=[(cDWN, cDWN.n)],
                            incs=[cADD] + ([cDCm] if tau == 3 else []))
                        pd_rd[id(pd)] = [cADD.back(0)]
                        t_g = s2 * 4 + tau
                        if ln2_sums:
                            B.op("scalar", lambda e, t=t_g, tau=tau: e.activation(
                                sq1a[:], x1a[:, tau], AF.Identity,
                                accum_out=sum2[:, t:t + 1]),
                                waits=[cADD], incs=[cSTA])
                            B.op("scalar", lambda e, t=t_g, tau=tau: e.activation(
                                sq1a[:], x1a[:, tau], AF.Square,
                                accum_out=ssq2[:, t:t + 1]),
                                waits=[cADD], incs=[cSTA])
                        B.op("gpsimd", lambda e, t=t_g, tau=tau, y=y_dst: e.dma_start(
                            out=y[t * 128:(t + 1) * 128, :], in_=x1a[:, tau]),
                            waits=[cADD] + ([cSTA] if ln2_sums else []),
                            incs=[(tY, 16)], dma=True)
            c_glob[0] += 12

        XN_T = {}
        norm_block(0, first=True)
        XN_T[0] = cTE.back(0)
        for s2 in range(nst):
            half_block(s2, 0)
            if s2 + 1 < nst:
                norm_block(s2 + 1)
                XN_T[s2 + 1] = cTE.back(0)
            half_block(s2, 1)

    B.barrier()
    import os as _os
    nst_lim = int(_os.environ.get("NSTLIM", str(NST)))
    if upto >= 2:
        ffn_pass(x_in, x1_dram, nst_lim, nmu1, rst1m, nb1m, True, g("tX1"),
                 nb_wait=lambda t: [(g("cNB"), 1 if t < NTILE // 2 else 0)])

    B.barrier()
    if upto < 3:
        B.op("gpsimd", lambda e: e.engine_nop(),
             waits=[(g("tX1"), g("tX1").n)] + [(t, t.n) for t in g("tWT")])
        return _emit(nc, B, ctx)
    stat_finalize(sum2, ssq2, nmu2, rst2, NTILE)
    B.op("vector", lambda e: e.tensor_mul(sfold[:], rst2[:], rst2[:]),
         waits=[g("cRSD")])
    B.op("vector", lambda e: e.tensor_mul(sfold[:], stat_t[:], sfold[:]),
         incs=[g("cSF1")])
    B.op("scalar", lambda e: e.activation(sfold[:], sfold[:], AF.Ln, bias=eps_t[:, 0:1]),
         waits=[g("cSF1")])
    B.op("scalar", lambda e: e.activation(sfold[:], sfold[:], AF.Exp, scale=-0.5),
         incs=[g("cSF2")])
    B.op("vector", lambda e: e.tensor_mul(sfold[:], rst2[:], sfold[:]),
         waits=[g("cSF2")], incs=[g("cRSD")])
    B.op("vector", lambda e: e.tensor_mul(nbf[:], nmu2[:], sfold[:]), incs=[g("cNB")])

    B.op("gpsimd", lambda e: e.dma_start(
        out=binrow[0:1, :], in_=b_in.rearrange("(a f) -> a f", a=1)),
        incs=[(g("sCB"), 16)], dma=True)
    for cn in range(8):
        B.op("gpsimd", lambda e, cn=cn: e.dma_start(
            out=win_sb[:, cn], in_=W_in[cn * 128:(cn + 1) * 128, :]),
            incs=[(g("sWI"), 16)], dma=True)
    WIN_LD = g("sWI").n

    cNRM, cTP, cTE = g("cNRM"), g("cTP"), g("cTE")
    cPE1, cVE, cKQ = g("cPE1"), g("cVE"), g("cKQ")
    nrm_done = {}
    xni_tau = {}
    bank_reader = {}

    def p1b_norm(sx):
        own = sx < NST_OWN
        for tau in range(4):
            t = sx * 4 + tau
            nd = nrm_done.get(t - 4) or []
            B.op("sync", lambda e, t=t: e.dma_start(
                out=x1r[:, t % 4], in_=x1_dram[t * 128:(t + 1) * 128, :]),
                waits=[(g("tX1"), g("tX1").n)] + list(nd),
                incs=[(g("sQ")[t % 4], 16)])
        for tau in range(4):
            t = sx * 4 + tau
            sl = tau % 2
            B.op("scalar", lambda e, t=t, sl=sl: e.activation(
                zn_tm[:, sl], x1r[:, t % 4], AF.Identity,
                scale=sfold[:, t:t + 1], bias=nbf[:, t:t + 1]),
                waits=[g("sQ")[t % 4], g("cNB"),
                       (cTP, max(0, cTP.n - 1))], incs=[cNRM])
            if own:
                B.op("vector", lambda e, t=t: e.tensor_scalar(
                    xm_tmp[:], x1r[:, t % 4], nmu2[:, t:t + 1], rst2[:, t:t + 1],
                    ALU.add, ALU.mult),
                    waits=[g("sQ")[t % 4], g("cRSD")])
                B.op("vector", lambda e, t=t: e.tensor_tensor(
                    xm_stg[:, t % 2], x1r[:, t % 4], xm_tmp[:], ALU.add),
                    waits=[g("tM")[t % 2]], incs=[cNRM])
                B.op("gpsimd", lambda e, t=t: e.dma_start(
                    out=xm_dram[t * 128:(t + 1) * 128, :], in_=xm_stg[:, t % 2]),
                    waits=[cNRM], incs=[(g("tM")[t % 2], 16)], dma=True)
            nrm_done[t] = cNRM.all()
            pt = pTv[sl]
            for fc in range(8):
                B.op("tensor", lambda e, sl=sl, fc=fc, pt=pt: e.transpose(
                    pt[:].rearrange("p (f j) -> p f j", f=8)[:, fc],
                    zn_tm[:, sl, fc * 128:(fc + 1) * 128], ident[:]),
                    waits=([cNRM, (cTE, 1)] if fc == 0 else []),
                    incs=([cTP] if fc == 7 else []))
            for fc in range(8):
                B.op("vector", lambda e, sx=sx, tau=tau, fc=fc, pt=pt: e.tensor_scalar(
                    xnifm[:, sx % 2, fc, tau * 128:(tau + 1) * 128],
                    pt[:].rearrange("p (f j) -> p f j", f=8)[:, fc],
                    lng_t[:, fc:fc + 1], lnb_t[:, fc:fc + 1], ALU.mult, ALU.add),
                    waits=([cTP, (cPE1, cPE1.n)] if fc == 0 else []),
                    incs=([cTE] if fc == 7 else []))
            xni_tau[(sx, tau)] = cTE.back(0)

    def p1b_mm(sx):
        own = sx < NST_OWN
        xb = sx % 2
        def brd(bank):
            r = bank_reader.get(id(bank))
            return [r] if r else []
        for tau in range(4):
            for eg in range(4):
                pv_ = pA if (tau * 4 + eg) % 2 == 0 else pB
                for fc in range(8):
                    B.op("tensor", lambda e, xb=xb, tau=tau, eg=eg, fc=fc, pv_=pv_: e.matmul(
                        pv_[:], xnifm[:, xb, fc, tau * 128:(tau + 1) * 128],
                        win_sb[:, fc, E + eg * 512:E + (eg + 1) * 512],
                        start=(fc == 0), stop=False),
                        waits=(([xni_tau[(sx, tau)], (g("sWI"), WIN_LD)]
                                + brd(pv_)) if fc == 0 else []))
                B.op("tensor", lambda e, eg=eg, pv_=pv_: e.matmul(
                    pv_[:], ones[0:1, 0:128],
                    binrow[0:1, E + eg * 512:E + (eg + 1) * 512],
                    start=False, stop=True),
                    waits=[(g("sCB"), g("sCB").n)], incs=[cPE1])
                B.op("scalar", lambda e, tau=tau, eg=eg, pv_=pv_: e.activation(
                    v_stg[:, tau % 2, eg * 512:(eg + 1) * 512], pv_[:], AF.Silu),
                    waits=[(cPE1, cPE1.n), g("tV")[tau % 2]],
                    incs=[cVE])
                bank_reader[id(pv_)] = cVE.back(0)
            B.op("gpsimd", lambda e, sx=sx, tau=tau: e.dma_start(
                out=v_dram[:, (sx * 4 + tau) * E:(sx * 4 + tau + 1) * E],
                in_=v_stg[:, tau % 2]), waits=[cVE], incs=[(g("tV")[tau % 2], 16)], dma=True)
        zsl = sx % 2
        for fc in range(8):
            B.op("tensor", lambda e, xb=xb, fc=fc: e.matmul(
                pA[:], win_sb[:, fc, 2 * E:2 * E + S], xnifm[:, xb, fc],
                start=(fc == 0), stop=(fc == 7)),
                waits=(brd(pA) if fc == 0 else []),
                incs=([cPE1] if fc == 7 else []))
        B.op("scalar", lambda e, zsl=zsl: e.activation(
            z_scr[:, zsl], pA[:], AF.Silu, bias=bz_t[:, 0:1]),
            waits=[(cPE1, cPE1.n), cKQ], incs=[cVE])
        bank_reader[id(pA)] = cVE.back(0)
        B.op("vector", lambda e, zsl=zsl: e.tensor_scalar(
            k_stg[:, zsl], z_scr[:, zsl], gk_t[:, 0:1], bk_t[:, 0:1],
            ALU.mult, ALU.add),
            waits=[cVE, g("tK")[zsl]], incs=[cKQ])
        B.op("gpsimd", lambda e, sx=sx, zsl=zsl: e.dma_start(
            out=k_dram[:, sx * 512:(sx + 1) * 512], in_=k_stg[:, zsl]),
            waits=[cKQ], incs=[(g("tK")[zsl], 16)], dma=True)
        if own:
            B.op("vector", lambda e, sx=sx, zsl=zsl: e.tensor_scalar(
                q_sb[:, sx * 512:(sx + 1) * 512], z_scr[:, zsl], gq_t[:, 0:1],
                bq_t[:, 0:1], ALU.mult, ALU.add), incs=[cKQ])
            for ecn in range(16):
                pu = pA if ecn % 2 == 0 else pB
                for fc in range(8):
                    B.op("tensor", lambda e, xb=xb, ecn=ecn, fc=fc, pu=pu: e.matmul(
                        pu[:], win_sb[:, fc, ecn * 128:(ecn + 1) * 128],
                        xnifm[:, xb, fc], start=(fc == 0), stop=(fc == 7)),
                        waits=(([cKQ] if ecn == 0 else []) + (brd(pu) if fc == 0 else [])),
                        incs=([cPE1] if fc == 7 else []))
                B.op("scalar", lambda e, ecn=ecn, pu=pu: e.activation(
                    u_stg[:, ecn % 4], pu[:], AF.Silu, bias=bu_t[:, ecn:ecn + 1]),
                    waits=[(cPE1, cPE1.n), g("tU")[ecn % 4]],
                    incs=[cVE])
                bank_reader[id(pu)] = cVE.back(0)
                B.op("gpsimd", lambda e, sx=sx, ecn=ecn: e.dma_start(
                    out=u_dram[ecn * 128:(ecn + 1) * 128, sx * 512:(sx + 1) * 512],
                    in_=u_stg[:, ecn % 4]), waits=[cVE], incs=[(g("tU")[ecn % 4], 16)], dma=True)

    p1b_norm(0)
    for sx in range(NST):
        if sx + 1 < NST:
            p1b_norm(sx + 1)
        p1b_mm(sx)

    if upto < 4:
        B.op("gpsimd", lambda e: e.engine_nop(),
             waits=[(t, t.n) for t in g("tV") + g("tK") + g("tU") + g("tM")])
        return _emit(nc, B, ctx)
    KB, NKB, CPB = 1024, 16, 8
    cSC, cRL, cP, cPV, cOE = g("cSC"), g("cRL"), g("cP"), g("cPV"), g("cOE")
    cOU, cY, cX2, cSTA = g("cOU"), g("cY"), g("cX2"), g("cSTA")

    for half in range(2):
        B.barrier()
        QOFF = half * 1024
        pdone = {}

        def emit_scores(kb, QOFF=None):
            base = (kb % 2) * CPB
            B.op("sync", lambda e, kb=kb: e.dma_start(
                out=kst[:, kb % 2], in_=k_dram[:, kb * KB:(kb + 1) * KB]),
                waits=[(t, t.n) for t in g("tK")] + [(cP, 16)],
                incs=[(g("sK")[kb % 2], 16)])
            B.op("sync", lambda e, kb=kb, base=base: e.dma_start(
                out=vring[:, base:base + CPB].rearrange("p c e -> p (c e)"),
                in_=v_dram[:, kb * CPB * E:(kb + 1) * CPB * E]),
                waits=[(t, t.n) for t in g("tV")] + [(cPV, cPV.n)],
                incs=[(g("sV")[kb % 2], 16)])
            psl = kb % 2
            for kt in range(CPB):
                for qs in range(2):
                    ps_ = pA if qs == 0 else pB
                    B.op("tensor", lambda e, kb=kb, kt=kt, qs=qs, ps_=ps_, QOFF=QOFF: e.matmul(
                        ps_[:], kst[:, kb % 2, kt * 128:(kt + 1) * 128],
                        q_sb[:, QOFF + qs * 512:QOFF + (qs + 1) * 512],
                        start=True, stop=True),
                        waits=([g("sK")[kb % 2], g("cKQ"), (cRL, 1)]
                               if (kt == 0 and qs == 0) else [(cRL, 1)]),
                        incs=[cSC])
                    rsl = (kt % 2) * 2 + qs
                    B.op("vector", lambda e, rsl=rsl, ps_=ps_: e.tensor_scalar_max(
                        rl_scr[:, rsl], ps_[:], 0.0),
                        waits=[(cSC, cSC.n), (cP, 3)], incs=[cRL])
                    B.op("scalar", lambda e, psl=psl, kt=kt, qs=qs, rsl=rsl: e.activation(
                        p_sb[:, psl, kt, qs * 512:(qs + 1) * 512], rl_scr[:, rsl],
                        AF.Square),
                        waits=[cRL, (cPV, cPV.n)], incs=[cP])
            pdone[kb] = cP.back(0)

        def emit_pv(kb):
            base = (kb % 2) * CPB
            psl = kb % 2
            pvslots = [pC[:, 0:512], pC[:, 512:1024], pD[:, 0:512], pD[:, 512:1024]]
            for ecn in range(16):
                for qs in range(2):
                    slot = pvslots[(ecn * 2 + qs) % 4]
                    for kt in range(CPB):
                        B.op("tensor", lambda e, base=base, ecn=ecn, qs=qs, kt=kt, slot=slot, psl=psl: e.matmul(
                            slot, vring[:, base + kt, ecn * 128:(ecn + 1) * 128],
                            p_sb[:, psl, kt, qs * 512:(qs + 1) * 512],
                            start=(kt == 0), stop=(kt == CPB - 1)),
                            waits=([pdone[kb], g("sV")[kb % 2],
                                    (cOE, 3)] if kt == 0 else []),
                            incs=([cPV] if kt == CPB - 1 else []))
                    if kb == 0:
                        B.op("vector", lambda e, ecn=ecn, qs=qs, slot=slot: e.tensor_copy(
                            o_sb[:, ecn, qs * 512:(qs + 1) * 512], slot),
                            waits=[(cPV, cPV.n)], incs=[cOE])
                    else:
                        B.op("vector", lambda e, ecn=ecn, qs=qs, slot=slot: e.tensor_tensor(
                            o_sb[:, ecn, qs * 512:(qs + 1) * 512],
                            o_sb[:, ecn, qs * 512:(qs + 1) * 512], slot, ALU.add),
                            waits=[(cPV, cPV.n)], incs=[cOE])

        emit_scores(0, QOFF=QOFF)
        for kb in range(NKB):
            if kb + 1 < NKB:
                emit_scores(kb + 1, QOFF=QOFF)
            emit_pv(kb)
        OE_ALL = cOE.back(0)
        for cn in range(16):
            B.op("gpsimd", lambda e, cn=cn: e.dma_start(
                out=wout_sb[:, cn], in_=W_out[cn * 128:(cn + 1) * 128, :]),
                waits=([(cPV, cPV.n - 32)] if cn == 0 else []),
                incs=[(g("sWO"), 16)], dma=True)
        WOUT_LD = g("sWO").n
        for ecn in range(16):
            B.op("sync", lambda e, ecn=ecn, QOFF=QOFF: e.dma_start(
                out=u_ld[:, ecn % 2],
                in_=u_dram[ecn * 128:(ecn + 1) * 128, QOFF:QOFF + 1024]),
                waits=[(t, t.n) for t in g("tU")] + [(cOU, 1), (cPV, cPV.n - 32)],
                incs=[(g("sU")[ecn % 2], 16)])
            B.op("vector", lambda e, ecn=ecn: e.tensor_tensor(
                ou_bf[:, ecn], u_ld[:, ecn % 2], o_sb[:, ecn], ALU.mult),
                waits=[g("sU")[ecn % 2], (cOE, 30 - 2 * ecn), (cPV, cPV.n)],
                incs=[cOU])
        OU_ALL = cOU.back(0)
        for qp in range(4):
            for tau, pd in ((0, pC), (1, pD)):
                qt = qp * 2 + tau
                tq = half * 8 + qt
                for ecn in range(16):
                    for fh in range(2):
                        B.op("tensor", lambda e, ecn=ecn, qt=qt, fh=fh, pd=pd: e.matmul(
                            pd[:, fh * 512:(fh + 1) * 512],
                            ou_bf[:, ecn, qt * 128:(qt + 1) * 128],
                            wout_sb[:, ecn, fh * 512:(fh + 1) * 512],
                            start=(ecn == 0), stop=False),
                            waits=(([(cOU, 15 - ecn)] if fh == 0 else [])
                                   + ([(g("sWO"), WOUT_LD), (cX2, 1)]
                                      if (ecn == 0 and fh == 0) else [])))
                for fh in range(2):
                    B.op("tensor", lambda e, fh=fh, pd=pd: e.matmul(
                        pd[:, fh * 512:(fh + 1) * 512], ones[0:1, 0:128],
                        bout_bf[0:1, fh * 512:(fh + 1) * 512], start=False, stop=True),
                        incs=([cY] if fh == 1 else []))
                B.op("sync", lambda e, tq=tq, qt=qt: e.dma_start(
                    out=xm_ld[:, qt % 2], in_=xm_dram[tq * 128:(tq + 1) * 128, :]),
                    waits=[(t, t.n) for t in g("tM")] + [(cX2, 1)],
                    incs=[(g("sM")[qt % 2], 16)])
                B.op("vector", lambda e, qt=qt, pd=pd: e.tensor_tensor(
                    x2_stg[:, qt % 2], xm_ld[:, qt % 2], pd[:], ALU.add),
                    waits=[g("sM")[qt % 2], (cY, cY.n),
                           g("tX2")[qt % 2]], incs=[cX2])
                B.op("vector", lambda e, tq=tq, qt=qt: e.reduce_sum(
                    sum3[:, tq:tq + 1], x2_stg[:, qt % 2], axis=AX), incs=[cSTA])
                B.op("scalar", lambda e, tq=tq, qt=qt: e.activation(
                    sq3_scr[:], x2_stg[:, qt % 2], AF.Square,
                    accum_out=ssq3[:, tq:tq + 1]), waits=[cX2], incs=[cSTA])
                B.op("gpsimd", lambda e, tq=tq, qt=qt: e.dma_start(
                    out=x2_dram[tq * 128:(tq + 1) * 128, :], in_=x2_stg[:, qt % 2]),
                    waits=[cX2, cSTA], incs=[(g("tX2")[qt % 2], 16)], dma=True)

    B.barrier()
    if upto < 5:
        B.op("gpsimd", lambda e: e.engine_nop(),
             waits=[(t, t.n) for t in g("tX2")])
        return _emit(nc, B, ctx)
    B.op("sync", lambda e: e.dma_start(
        out=wgT[:].rearrange("p c h -> p (c h)"), in_=wgT_dram),
        incs=[(g("sR"), 16)])
    B.op("sync", lambda e: e.dma_start(
        out=wuT[:].rearrange("p c h -> p (c h)"), in_=wuT_dram),
        incs=[(g("sR"), 16)])
    W_RELOAD = g("sR").n
    stat_finalize(sum3, ssq3, nmu3, rst3, 16, nbt=nb3, m127=(rst3m, nb3m))
    ffn_pass(x2_dram, out, NST_OWN, nmu3, rst3m, nb3m, False, g("tOUT"),
             first_gate_waits=[(g("sR"), W_RELOAD)])

    B.op("gpsimd", lambda e: e.engine_nop(), waits=[(g("tOUT"), g("tOUT").n)])
    return _emit(nc, B, ctx)


def _emit(nc, B, ctx):
    with nc.Block() as blk:
        @blk.sync
        def _(e):
            for f in B.streams["sync"]:
                f(e)

        @blk.gpsimd
        def _(e):
            for f in B.streams["gpsimd"]:
                f(e)

        @blk.tensor
        def _(e):
            for f in B.streams["tensor"]:
                f(e)

        @blk.vector
        def _(e):
            for f in B.streams["vector"]:
                f(e)

        @blk.scalar
        def _(e):
            for f in B.streams["scalar"]:
                f(e)

    ctx.close()
    return nc


def kernel(**inputs):
    x = np.asarray(inputs["L0_attn_x"]).reshape(T_ALL, D)
    nc = build_nc(dbg=False)
    in_maps = []
    for c in range(NC_N):
        m = {k: np.ascontiguousarray(np.asarray(v)) for k, v in inputs.items()
             if k != "L0_attn_x"}
        m["L0_attn_x"] = np.ascontiguousarray(np.roll(x, -T_OWN * c, axis=0))
        in_maps.append(m)
    res = run_bass_kernel_spmd(nc, in_maps, core_ids=list(range(NC_N)))
    o = np.empty((T_ALL, D), dtype=np.float32)
    for c in range(NC_N):
        o[c * T_OWN:(c + 1) * T_OWN] = res.results[c]["out"]
    return o.reshape(1, T_ALL, D)
